# version 18
# baseline (speedup 1.0000x reference)
import math
from contextlib import ExitStack

import numpy as np
import ml_dtypes

import concourse.bass as bass
import concourse.mybir as mybir
from concourse.bass_utils import run_bass_kernel_spmd

F32 = mybir.dt.float32
BF16 = mybir.dt.bfloat16
AF = mybir.ActivationFunctionType
ALU = mybir.AluOpType
AX = mybir.AxisListType

H = 16
HK = 4
DM = 2048
DFF = 8192
TKV = 2048
TQ = 1024
NKT = 16
NQT = 8
NSLOT = 9
SCALE = 1.0 / math.sqrt(128.0)
EPS_LN = 1e-5
EPS_QK = 1e-6
PART_BYTES = 200 * 1024
NSC = 128


class P:
    def __init__(self, nc, ctx, names):
        self.nc = nc
        self.ctx = ctx
        self.s = {n: ctx.enter_context(nc.semaphore(n)) for n in names}
        self.c = {n: 0 for n in names}
        self.marks = {}

    def _ensure(self, name):
        if name not in self.s:
            self.s[name] = self.ctx.enter_context(self.nc.semaphore(name))
            self.c[name] = 0

    def inc(self, inst, name, k=1):
        self._ensure(name)
        self.c[name] += k
        inst.then_inc(self.s[name], k)
        return self.c[name]

    def w(self, eng, name, val=None):
        self._ensure(name)
        v = self.c[name] if val is None else val
        if v > 0:
            eng.wait_ge(self.s[name], v)

    def mark(self, key, name):
        self.marks[key] = (name, self.c[name])

    def wm(self, eng, key):
        name, v = self.marks[key]
        if v > 0:
            eng.wait_ge(self.s[name], v)


class Arena:

    def __init__(self, nc, ctx):
        self.t = ctx.enter_context(
            nc.sbuf_tensor("arena", [128, PART_BYTES], mybir.dt.uint8))
        self.claims = []

    def carve(self, name, cols, dtype, phases):
        esz = 4 if dtype == F32 else 2
        size = cols * esz
        phases = frozenset(phases)
        taken = sorted((o, s) for (o, s, ph, _n) in self.claims if ph & phases)
        off = 0
        for (o, s) in taken:
            if off + size <= o:
                break
            off = max(off, ((o + s + 63) // 64) * 64)
        assert off + size <= PART_BYTES, f"arena overflow at {name}"
        self.claims.append((off, size, phases, name))
        ap = self.t.ap().bitcast(dtype)
        return ap[:, off // esz: off // esz + cols]


class _Stop(Exception):
    pass


ARENA = {}


def build_program(upto=99):
    nc = bass.Bass("TRN2", target_bir_lowering=False)

    xd = nc.dram_tensor("x", [TKV, DM], F32, kind="ExternalInput").ap()
    wqTd = nc.dram_tensor("wqT", [DM, 2048], F32, kind="ExternalInput").ap()
    wkTd = nc.dram_tensor("wkT", [DM, 512], F32, kind="ExternalInput").ap()
    wvTd = nc.dram_tensor("wvT", [DM, 512], F32, kind="ExternalInput").ap()
    woTd = nc.dram_tensor("woT", [2048, DM], F32, kind="ExternalInput").ap()
    w1d = nc.dram_tensor("w1", [DFF, DM], F32, kind="ExternalInput").ap()
    w1pd = nc.dram_tensor("w1p", [DFF, DM], F32, kind="ExternalInput").ap()
    w2Td = nc.dram_tensor("w2T", [DFF, DM], F32, kind="ExternalInput").ap()
    b1sd = nc.dram_tensor("b1s", [128, 64], F32, kind="ExternalInput").ap()
    b2sd = nc.dram_tensor("b2s", [128, 16], F32, kind="ExternalInput").ap()
    gamqd = nc.dram_tensor("gamq", [128, 1], F32, kind="ExternalInput").ap()
    gamkd = nc.dram_tensor("gamk", [128, 1], F32, kind="ExternalInput").ap()
    gamqrd = nc.dram_tensor("gamqr", [1, 128], F32, kind="ExternalInput").ap()
    gamkrd = nc.dram_tensor("gamkr", [1, 128], F32, kind="ExternalInput").ap()
    identd = nc.dram_tensor("ident", [128, 128], BF16, kind="ExternalInput").ap()
    rtd = nc.dram_tensor("rt", [128, 128], BF16, kind="ExternalInput").ap()
    onesmd = nc.dram_tensor("onesm", [128, 128], BF16, kind="ExternalInput").ap()
    onescd = nc.dram_tensor("onesc", [128, 1], BF16, kind="ExternalInput").ap()
    onescfd = nc.dram_tensor("onescf", [128, 1], F32, kind="ExternalInput").ap()
    onesrfd = nc.dram_tensor("onesrf", [1, 128], F32, kind="ExternalInput").ap()
    cosqd = nc.dram_tensor("cosq", [128, TQ], F32, kind="ExternalInput").ap()
    sinqd = nc.dram_tensor("sinq", [128, TQ], F32, kind="ExternalInput").ap()
    coskd = nc.dram_tensor("cosk", [128, TKV], F32, kind="ExternalInput").ap()
    sinkd = nc.dram_tensor("sink", [128, TKV], F32, kind="ExternalInput").ap()
    m01d = nc.dram_tensor("m01", [128, NQT * NSLOT * 128], BF16,
                          kind="ExternalInput").ap()

    x2scr = nc.dram_tensor("x2scr", [TQ, DM], F32, kind="Internal").ap()
    wqbf = nc.dram_tensor("wqbf", [DM, 2048], BF16, kind="Internal").ap()
    wkbf = nc.dram_tensor("wkbf", [DM, 512], BF16, kind="Internal").ap()
    wvbf = nc.dram_tensor("wvbf", [DM, 512], BF16, kind="Internal").ap()
    wobf = nc.dram_tensor("wobf", [2048, DM], BF16, kind="Internal").ap()
    outd = nc.dram_tensor("out", [TQ, DM], F32, kind="ExternalOutput").ap()

    ctx = ExitStack()
    try:
      with ctx:
        ar = Arena(nc, ctx)
        SCP = {"A", "B1", "B2K", "B2Q", "C", "D"}
        B2 = {"B2K", "B2Q"}
        xln1T = ar.carve("xln1T", 32768, BF16, {"A", "B1"} | B2)
        out_acc = ar.carve("out_acc", 16384, F32, {"E", "F"})
        Q = ar.carve("Q", 16384, BF16, {"B2Q", "C"})
        K = ar.carve("K", 8192, BF16, B2 | {"C"})
        V = ar.carve("V", 8192, BF16, {"B1", "C"} | B2)
        attnout = ar.carve("attnout", 16384, BF16, {"C", "D"})
        xln2T = ar.carve("xln2T", 16384, BF16, {"D", "E"})
        masks = ar.carve("masks", 9216, BF16, {"C"})
        scw = [ar.carve(f"scw{j}", 1024, F32, SCP) for j in range(4)]
        scdump = ar.carve("scdump", 1024, F32, SCP)
        xbuf = [ar.carve(f"xbuf{j}", 2048, F32, {"A"}) for j in range(2)]
        wcf = [ar.carve(f"wcf{j}", 2048, F32, {"A"}) for j in range(2)]
        wcb = [ar.carve(f"wcb{j}", 2048, BF16, {"A"}) for j in range(2)]
        xlntm = [ar.carve(f"xlntm{j}", 2048, BF16, {"A"}) for j in range(2)]
        wvTs = ar.carve("wvTs", 8192, BF16, {"B1"})
        cosk = ar.carve("cosk", 2048, F32, {"B2K"})
        sink = ar.carve("sink", 2048, F32, {"B2K"})
        cosq = ar.carve("cosq", 1024, F32, {"B2Q"})
        sinq = ar.carve("sinq", 1024, F32, {"B2Q"})
        wst = [ar.carve(f"wst{j}", 2048, BF16, B2) for j in range(2)]
        sA = ar.carve("sA", 2048, F32, B2)
        sB = ar.carve("sB", 2048, F32, B2)
        sC = ar.carve("sC", 2048, F32, B2)
        sBF = ar.carve("sBF", 2048, BF16, B2)
        ptraw = [ar.carve(f"ptraw{j}", 1152, BF16, {"C"}) for j in range(2)]
        den_sb = ar.carve("den_sb", 1024, F32, {"C"})[0:1, :]
        rden_sb = ar.carve("rden_sb", 1024, F32, {"C"})[0:1, :]
        ptm = [ar.carve(f"ptm{j}", 1152, BF16, {"C"}) for j in range(2)]
        xr = [ar.carve(f"xr{j}", 2048, F32, {"D"}) for j in range(2)]
        ytile = [ar.carve(f"ytile{j}", 2048, F32, {"D"}) for j in range(2)]
        x2t = [ar.carve(f"x2t{j}", 2048, F32, {"D"}) for j in range(2)]
        xl2tm = [ar.carve(f"xl2tm{j}", 2048, BF16, {"D"}) for j in range(2)]
        wotb = [ar.carve(f"wotb{j}", 2048, BF16, {"D"}) for j in range(2)]
        ybuf = [ar.carve(f"ybuf{j}", 2048, F32, {"D"}) for j in range(2)]
        w1b = [ar.carve(f"w1b{j}", 2048, F32, {"E"}) for j in range(2)]
        w2b = [ar.carve(f"w2b{j}", 2048, F32, {"E"}) for j in range(2)]
        w1qn = [ar.carve(f"w1qn{j}", 2048, BF16, {"E"}) for j in range(2)]
        a1m = ar.carve("a1m", 2048, BF16, {"E"})
        a2m = ar.carve("a2m", 2048, BF16, {"E"})
        w2qn = [ar.carve(f"w2qn{j}", 2048, BF16, {"E"}) for j in range(8)]
        h_buf = ar.carve("h_buf", 8192, BF16, {"E"})
        x2b = [ar.carve(f"x2b{j}", 2048, F32, {"F"}) for j in range(2)]
        acb = [ar.carve(f"acb{j}", 2048, BF16, {"F"}) for j in range(2)]
        outb = [ar.carve(f"outb{j}", 2048, F32, {"F"}) for j in range(2)]
        ARENA.clear()
        for (o, s, ph, n) in ar.claims:
            ARENA[n] = (o, s)

        def sv(name, cols, dtype=F32):
            return ctx.enter_context(
                nc.sbuf_tensor(name, [128, cols], dtype)).ap()

        ident = sv("identsb", 128, BF16)
        rt = sv("rtsb", 128, BF16)
        onesm = sv("onesmsb", 128, BF16)
        onesc = sv("onescsb", 1, BF16)
        onescf = sv("onescfsb", 1)
        gamq = sv("gamqsb", 1)
        gamk = sv("gamksb", 1)
        b1sb = sv("b1sbs", 64)
        b2sb = sv("b2sbs", 16)
        biasB = sv("biasBs", 1)
        theta1 = sv("theta1s", 1)
        theta2 = sv("theta2s", 1)
        negth1 = sv("negth1s", 1)
        negth2 = sv("negth2s", 1)
        sstr1 = sv("sstr1s", NSC)
        sstr2 = sv("sstr2s", NSC)
        st6 = sv("st6s", 24)
        mvv = [sv(f"mv{j}s", 2) for j in range(2)]
        stdv = sv("stdvs", 2)
        rstdv = sv("rstdvs", 2)
        tiny = sv("tinys", 8)
        epsln = sv("epslns", 1)
        epsqk = sv("epsqks", 1)

        onesrf = ctx.enter_context(
            nc.sbuf_tensor("onesrfsb", [1, 128], F32)).ap()
        gqr = ctx.enter_context(nc.sbuf_tensor("gqrsb", [1, 128], F32)).ap()
        gkr = ctx.enter_context(nc.sbuf_tensor("gkrsb", [1, 128], F32)).ap()

        trow = ctx.enter_context(nc.sbuf_tensor("trowsb", [1, 8], F32)).ap()

        p = P(nc, ctx, ["sP", "sV", "sA", "sPo"])
        block = ctx.enter_context(nc.Block())

        w1flat = w1d.rearrange("a b -> (a b)").rearrange(
            "(c p n) -> c p n", p=128, n=1024)
        w2flat = w2Td.rearrange("a b -> (a b)").rearrange(
            "(c p n) -> c p n", p=128, n=1024)
        sc = {"ld": 0, "act": 0}
        NSC2 = 2 * NSC

        def scale_loads(sync, n):
            for _ in range(n):
                c = sc["ld"]
                if c >= NSC2:
                    return
                if c >= 4 and ("scA", c - 4) not in p.marks:
                    return
                sc["ld"] = c + 1
                if c >= 4:
                    p.wm(sync, ("scA", c - 4))
                srcp = w1flat[c] if c < NSC else w2flat[c - NSC]
                p.inc(sync.dma_start(out=scw[c % 4], in_=srcp),
                      f"scl{c % 4}", 16)
                p.mark(("scld", c), f"scl{c % 4}")

        def scale_pool(g, n):
            return

        def scale_act(s, n):
            return

        def scale_dve(v, n):
            for _ in range(n):
                c = sc["act"]
                if c >= sc["ld"]:
                    return
                sc["act"] = c + 1
                strip = sstr1 if c < NSC else sstr2
                p.wm(v, ("scld", c))
                p.w(v, "sV")
                p.inc(v.tensor_reduce(out=strip[:, c % NSC:c % NSC + 1],
                                      in_=scw[c % 4], axis=AX.X, op=ALU.add,
                                      apply_absolute_value=True), "sV")
                p.mark(("scA", c), "sV")


        def ln_stats(v, src_ap, mv):
            for q4 in range(4):
                p.w(v, "sV")
                p.inc(v.bn_stats(out=st6[:, q4 * 6:q4 * 6 + 6],
                                 in_=src_ap[:, q4 * 512:(q4 + 1) * 512]),
                      "sV")
            p.w(v, "sV")
            p.inc(v.bn_aggr(out=mv[:, 0:2], in_=st6[:, 0:24]), "sV")

        @block.sync
        def _(sync):
            for dst, src in [(ident, identd), (rt, rtd), (onesm, onesmd),
                             (onesc, onescd), (onescf, onescfd),
                             (gamq, gamqd), (gamk, gamkd),
                             (b1sb, b1sd), (b2sb, b2sd),
                             (gqr[0:1, :], gamqrd), (gkr[0:1, :], gamkrd),
                             (onesrf[0:1, :], onesrfd)]:
                p.inc(sync.dma_start(out=dst, in_=src), "cst", 16)
            p.mark("consts", "cst")

        with nc.psum_tensor("ps0", [128, 8], F32) as ps0t:
            ps0 = ps0t.ap()

            @block.vector
            def _(v):
                p.wm(v, "consts")
                p.inc(v.reduce_max(out=trow[0:1, 0:1], in_=gqr[0:1, :],
                                   axis=AX.X, apply_absolute_value=True), "sV")
                p.w(v, "sV")
                p.inc(v.reduce_max(out=trow[0:1, 1:2], in_=gkr[0:1, :],
                                   axis=AX.X, apply_absolute_value=True), "sV")
                p.w(v, "sV")
                p.inc(v.tensor_tensor(out=trow[0:1, 2:3], in0=trow[0:1, 0:1],
                                      in1=trow[0:1, 1:2], op=ALU.mult), "sV")
                p.w(v, "sV")
                p.inc(v.tensor_scalar(out=trow[0:1, 3:4], in0=trow[0:1, 2:3],
                                      scalar1=-SCALE, scalar2=None,
                                      op0=ALU.mult), "sV")
                p.mark("bBrow", "sV")

            @block.tensor
            def _(t):
                p.wm(t, "bBrow")
                p.inc(t.matmul(out=ps0[:, 0:1], lhsT=onesrf[0:1, :],
                               rhs=trow[0:1, 3:4], start=True, stop=True), "sP")
                p.mark("bBmm", "sP")

            @block.vector
            def _(v):
                p.wm(v, "bBmm")
                p.inc(v.tensor_copy(out=biasB, in_=ps0[:, 0:1]), "sV")
                p.mark("biasB", "sV")
                p.w(v, "sV")
                p.inc(v.memset(epsln, EPS_LN), "sV")
                p.mark("epsln", "sV")
                p.w(v, "sV")
                p.inc(v.memset(epsqk, EPS_QK), "sV")
                p.mark(("epsqk", 0), "sV")

        if upto < 1:
            raise _Stop
        cvt_jobs = []
        for c in range(16):
            cvt_jobs.append((woTd[c * 128:(c + 1) * 128, :],
                             wobf[c * 128:(c + 1) * 128, :], 2048, "wo", c))
        cvs = {"ld": 0, "cast": 0, "st": 0}

        def cvt_loads(sync, n):
            for _ in range(n):
                c = cvs["ld"]
                if c >= len(cvt_jobs):
                    return
                if c >= 2 and ("cvst", c - 2) not in p.marks:
                    return
                cvs["ld"] = c + 1
                srcv, dstv, w, tag, ci = cvt_jobs[c]
                if c >= 2:
                    p.wm(sync, ("cvst", c - 2))
                p.inc(sync.dma_start(out=wcf[c % 2][:, 0:w], in_=srcv),
                      f"wcl{c % 2}", 16)
                p.mark(("cvld", c), f"wcl{c % 2}")

        def cvt_casts(s, n):
            for _ in range(n):
                c = cvs["cast"]
                if c >= cvs["ld"]:
                    return
                cvs["cast"] = c + 1
                srcv, dstv, w, tag, ci = cvt_jobs[c]
                p.wm(s, ("cvld", c))
                p.w(s, "sA")
                p.inc(s.activation(out=wcb[c % 2][:, 0:w],
                                   in_=wcf[c % 2][:, 0:w], func=AF.Copy),
                      "sA")
                p.mark(("cvcast", c), "sA")

        def cvt_stores(sync, n):
            for _ in range(n):
                c = cvs["st"]
                if c >= cvs["cast"]:
                    return
                cvs["st"] = c + 1
                srcv, dstv, w, tag, ci = cvt_jobs[c]
                p.wm(sync, ("cvcast", c))
                p.inc(sync.dma_start(out=dstv, in_=wcb[c % 2][:, 0:w]),
                      f"wcs{c % 2}", 16)
                p.mark(("cvst", c), f"wcs{c % 2}")
                p.mark((f"cv_{tag}", ci), f"wcs{c % 2}")

        with nc.psum_tensor("psTrA0", [128, 2048], F32) as _a0, \
                nc.psum_tensor("psTrA1", [128, 2048], F32) as _a1:
            psTr = [_a0.ap(), _a1.ap()]
            for i in range(NKT):
                j = i % 2
                mv = mvv[j]

                @block.sync
                def _(sync, i=i, j=j):
                    if i >= 2:
                        p.wm(sync, ("Aapply", i - 2))
                    p.inc(sync.dma_start(out=xbuf[j],
                                         in_=xd[i * 128:(i + 1) * 128, :]),
                          f"xld{j}", 16)
                    p.mark(("Axld", i), f"xld{j}")
                    cvt_stores(sync, 1)
                    cvt_loads(sync, 1)
                    scale_loads(sync, 2)

                @block.vector
                def _(v, i=i, j=j, mv=mv):
                    p.w(v, "sV")
                    p.wm(v, ("Axld", i))
                    if i >= 2:
                        p.wm(v, ("Astd", i - 2))
                    ln_stats(v, xbuf[j], mv)
                    p.mark(("Amv", i), "sV")

                @block.scalar
                def _(s, i=i, j=j, mv=mv):
                    p.w(s, "sA")
                    p.wm(s, ("Amv", i))
                    p.inc(s.activation(out=stdv[:, j:j + 1], in_=mv[:, 1:2],
                                       func=AF.Sqrt, bias=epsln), "sA")
                    p.mark(("Astd", i), "sA")
                    cvt_casts(s, 1)

                @block.vector
                def _(v, i=i, j=j, mv=mv):
                    p.w(v, "sV")
                    p.wm(v, ("Astd", i))
                    p.inc(v.reciprocal(out=rstdv[:, j:j + 1],
                                       in_=stdv[:, j:j + 1]), "sV")
                    p.w(v, "sV")
                    if i >= 2:
                        p.wm(v, ("Atr", i - 2))
                    p.inc(v.tensor_scalar(out=xlntm[j], in0=xbuf[j],
                                          scalar1=mv[:, 0:1],
                                          scalar2=rstdv[:, j:j + 1],
                                          op0=ALU.subtract, op1=ALU.mult),
                          "sV")
                    p.mark(("Aapply", i), "sV")

                @block.tensor
                def _(t, i=i, j=j):
                    p.wm(t, ("Aapply", i))
                    if i >= 2:
                        p.wm(t, ("Aevac", i - 2))
                    inst = None
                    for c in range(16):
                        inst = t.matmul(
                            out=psTr[j][:, c * 128:(c + 1) * 128],
                            lhsT=xlntm[j][:, c * 128:(c + 1) * 128],
                            rhs=ident, start=True, stop=True)
                    p.inc(inst, "sP")
                    p.mark(("Atr", i), "sP")

                @block.vector
                def _(v, i=i, j=j):
                    p.w(v, "sV")
                    p.wm(v, ("Atr", i))
                    dst = xln1T.rearrange("p (c n) -> p c n", c=16)[
                        :, :, i * 128:(i + 1) * 128]
                    src = psTr[j].rearrange("p (c n) -> p c n", c=16)
                    p.inc(v.tensor_copy(out=dst, in_=src), "sV")
                    p.mark(("Aevac", i), "sV")
                    scale_dve(v, 2)

            while cvs["st"] < len(cvt_jobs):
                @block.sync
                def _(sync):
                    cvt_stores(sync, 2)
                    cvt_loads(sync, 2)

                @block.scalar
                def _(s):
                    cvt_casts(s, 2)

            p.mark("Aend_sV", "sV")
            p.mark("Aend_sP", "sP")
            p.mark("Aend_sA", "sA")

        if upto < 2:
            raise _Stop
        with nc.psum_tensor("psV0", [128, 512], F32) as _v0, \
                nc.psum_tensor("psV1", [128, 512], F32) as _v1:
            psV = [_v0.ap(), _v1.ap()]

            @block.gpsimd
            def _(g):
                p.wm(g, "Aend_sV")
                p.wm(g, "Aend_sP")
                p.wm(g, ("cv_wo", 15))
                p.wm(g, ("cv_wo", 14))
                srcv = wvTd.rearrange("(c p) e -> p c e", p=128)
                dstv = wvTs.rearrange("p (c e) -> p c e", c=16)
                p.inc(g.dma_start(out=dstv, in_=srcv), "wvl", 16)
                p.mark("wvld", "wvl")

            for m in range(NKT):
                @block.tensor
                def _(t, m=m):
                    if m == 0:
                        p.wm(t, "wvld")
                        p.wm(t, "Aend_sV")
                        p.wm(t, ("cv_wo", 15))
                        p.wm(t, ("cv_wo", 14))
                    if m >= 2:
                        p.wm(t, ("Vevac", m - 2))
                    inst = None
                    for dc in range(16):
                        inst = t.matmul(
                            out=psV[m % 2],
                            lhsT=xln1T[:, dc * 2048 + m * 128:
                                       dc * 2048 + (m + 1) * 128],
                            rhs=wvTs[:, dc * 512:(dc + 1) * 512],
                            start=(dc == 0), stop=(dc == 15))
                    p.inc(inst, "sP")
                    p.mark(("Vmm", m), "sP")

                @block.scalar
                def _(s, m=m):
                    p.w(s, "sA")
                    if m == 0:
                        p.wm(s, ("cv_wo", 15))
                        p.wm(s, ("cv_wo", 14))
                    p.wm(s, ("Vmm", m))
                    p.inc(s.activation(out=V[:, m * 512:(m + 1) * 512],
                                       in_=psV[m % 2], func=AF.Copy), "sA")
                    p.mark(("Vevac", m), "sA")

                @block.vector
                def _(v):
                    scale_dve(v, 2)

                @block.sync
                def _(sync):
                    scale_loads(sync, 2)

            p.mark("B1end_sP", "sP")
            p.mark("B1end_sA", "sA")

        if upto < 3:
            raise _Stop

        @block.sync
        def _(sync):
            p.wm(sync, "B1end_sP")
            p.wm(sync, "B1end_sA")
            p.wm(sync, "Aend_sV")
            for dst, srcd in [(cosk, coskd), (sink, sinkd)]:
                p.inc(sync.dma_start(out=dst, in_=srcd), "rtk", 16)
            p.mark("ropetab", "rtk")

        def hb1(idx, width, wTd, tag, pp, ppo, so, lag):
            nw = width // 512

            @block.gpsimd
            def _(g, idx=idx):
                if ("proj" + tag, idx - 2) in p.marks:
                    p.wm(g, ("proj" + tag, idx - 2))
                if tag == "Q" and idx < 2:
                    p.wm(g, ("projK", HK - 2 + idx))
                srcv = wTd[:, idx * 128:(idx + 1) * 128].rearrange(
                    "(c p) e -> p c e", p=128)
                dstw = wst[idx % 2].rearrange("p (c e) -> p c e", c=16)
                p.inc(g.dma_start(out=dstw, in_=srcv), f"wstl{idx % 2}", 16)
                p.mark(("wld" + tag, idx), f"wstl{idx % 2}")

            @block.tensor
            def _(t_, idx=idx):
                p.wm(t_, ("wld" + tag, idx))
                if idx == 0 and tag == "K":
                    p.wm(t_, "B1end_sA")
                    p.wm(t_, "ropetab")
                    p.wm(t_, "biasB")
                if idx == 0 and tag == "Q":
                    p.wm(t_, ("sqrtK", HK - 1))
                    p.wm(t_, ("appK", HK - 1))
                    p.wm(t_, ("nbK", HK - 1))
                    p.wm(t_, ("finK", HK - 1))
                if ("sqrt" + tag, idx - 2) in p.marks:
                    p.wm(t_, ("sqrt" + tag, idx - 2))
                if lag == 1 and ("t2" + tag, idx - 1) in p.marks:
                    p.wm(t_, ("t2" + tag, idx - 1))
                inst = None
                xoff = 0 if width == 2048 else 1024
                for dc in range(16):
                    for qq in range(nw):
                        inst = t_.matmul(
                            out=pp[:, qq * 512:(qq + 1) * 512],
                            lhsT=wst[idx % 2][:, dc * 128:(dc + 1) * 128],
                            rhs=xln1T[:, dc * 2048 + xoff + qq * 512:
                                      dc * 2048 + xoff + (qq + 1) * 512],
                            start=(dc == 0), stop=(dc == 15))
                p.inc(inst, "sP")
                p.mark(("proj" + tag, idx), "sP")

            @block.scalar
            def _(s, idx=idx):
                p.w(s, "sA")
                p.wm(s, ("proj" + tag, idx))
                for key in (("app" + tag, idx - lag), ("nb" + tag, idx - lag)):
                    if key in p.marks:
                        p.wm(s, key)
                p.inc(s.activation(out=sA[:, so:so + width],
                                   in_=pp[:, 0:width], func=AF.Copy), "sA")
                p.w(s, "sA")
                p.inc(s.activation(out=sBF[:, so:so + width],
                                   in_=sA[:, so:so + width], func=AF.Copy),
                      "sA")
                p.mark(("raw" + tag, idx), "sA")

            @block.tensor
            def _(t_, idx=idx):
                p.wm(t_, ("raw" + tag, idx))
                if lag == 1:
                    if ("sqrt" + tag, idx - 1) in p.marks:
                        p.wm(t_, ("sqrt" + tag, idx - 1))
                else:
                    if ("t2" + tag, idx - 2) in p.marks:
                        p.wm(t_, ("t2" + tag, idx - 2))
                inst = None
                for qq in range(nw):
                    inst = t_.matmul(
                        out=ppo[:, qq * 512:(qq + 1) * 512],
                        lhsT=rt, rhs=sBF[:, so + qq * 512:so + (qq + 1) * 512],
                        start=True, stop=True)
                p.inc(inst, "sP")
                p.mark(("rq" + tag, idx), "sP")

        def hb2a(idx, width, cos_t, sin_t, tag, pp, ppo, so):
            nw = width // 512

            if tag == "Q":
                @block.gpsimd
                def _(g, idx=idx):
                    p.w(g, "sPo")
                    p.wm(g, ("raw" + tag, idx))
                    if idx == 0:
                        p.wm(g, "ropetabq")
                    p.inc(g.tensor_tensor(out=sB[:, so:so + width],
                                          in0=sA[:, so:so + width],
                                          in1=cos_t, op=ALU.mult), "sPo")
                    p.mark(("t1" + tag, idx), "sPo")

                @block.vector
                def _(v, idx=idx):
                    p.w(v, "sV")
                    p.wm(v, ("t1" + tag, idx))
                    p.wm(v, ("rq" + tag, idx))
                    if idx == 0:
                        p.wm(v, "ropetabq")
                    p.inc(v.tensor_tensor(out=sA[:, so:so + width],
                                          in0=ppo[:, 0:width],
                                          in1=sin_t, op=ALU.mult), "sV")
                    p.mark(("t2" + tag, idx), "sV")
                    scale_dve(v, 2)

                @block.gpsimd
                def _(g, idx=idx):
                    p.w(g, "sPo")
                    p.wm(g, ("t2" + tag, idx))
                    if ("app" + tag, idx - 2) in p.marks:
                        p.wm(g, ("app" + tag, idx - 2))
                    p.inc(g.tensor_tensor(out=sC[:, so:so + width],
                                          in0=sB[:, so:so + width],
                                          in1=sA[:, so:so + width],
                                          op=ALU.add), "sPo")
                    p.mark(("ro" + tag, idx), "sPo")
            else:
                @block.vector
                def _(v, idx=idx):
                    p.w(v, "sV")
                    p.wm(v, ("raw" + tag, idx))
                    p.inc(v.tensor_tensor(out=sB[:, so:so + width],
                                          in0=sA[:, so:so + width],
                                          in1=cos_t, op=ALU.mult), "sV")
                    p.mark(("t1" + tag, idx), "sV")
                    p.wm(v, ("rq" + tag, idx))
                    p.w(v, "sV")
                    p.inc(v.tensor_tensor(out=sA[:, so:so + width],
                                          in0=ppo[:, 0:width],
                                          in1=sin_t, op=ALU.mult), "sV")
                    p.mark(("t2" + tag, idx), "sV")
                    p.w(v, "sV")
                    p.inc(v.tensor_tensor(out=sC[:, so:so + width],
                                          in0=sB[:, so:so + width],
                                          in1=sA[:, so:so + width],
                                          op=ALU.add), "sV")
                    p.mark(("ro" + tag, idx), "sV")
                    scale_dve(v, 2)

            @block.scalar
            def _(s, idx=idx):
                p.w(s, "sA")
                p.wm(s, ("ro" + tag, idx))
                p.inc(s.activation(out=sBF[:, so:so + width],
                                   in_=sC[:, so:so + width],
                                   func=AF.Square), "sA")
                p.mark(("sq" + tag, idx), "sA")

            @block.tensor
            def _(t_, idx=idx):
                p.wm(t_, ("sq" + tag, idx))
                p.wm(t_, ("t2" + tag, idx))
                inst = None
                for qq in range(nw):
                    inst = t_.matmul(
                        out=pp[:, qq * 512:(qq + 1) * 512],
                        lhsT=onesm,
                        rhs=sBF[:, so + qq * 512:so + (qq + 1) * 512],
                        start=True, stop=True)
                p.inc(inst, "sP")
                p.mark(("nb" + tag, idx), "sP")

            @block.scalar
            def _(s, idx=idx):
                p.w(s, "sA")
                p.wm(s, ("nb" + tag, idx))
                p.inc(s.activation(out=sA[:, so:so + width],
                                   in_=pp[:, 0:width], func=AF.Sqrt), "sA")
                p.mark(("sqrt" + tag, idx), "sA")

        def hb2b(idx, width, gam_t, dst, dst_off, tag, so):
            @block.vector
            def _(v, idx=idx):
                p.w(v, "sV")
                p.wm(v, ("sqrt" + tag, idx))
                p.inc(v.tensor_scalar(out=sB[:, so:so + width],
                                      in0=sA[:, so:so + width],
                                      scalar1=EPS_QK, scalar2=None,
                                      op0=ALU.add), "sV")
                p.w(v, "sV")
                p.inc(v.reciprocal(out=sA[:, so:so + width],
                                   in_=sB[:, so:so + width]), "sV")
                p.w(v, "sV")
                p.inc(v.tensor_tensor(out=sB[:, so:so + width],
                                      in0=sC[:, so:so + width],
                                      in1=sA[:, so:so + width], op=ALU.mult),
                      "sV")
                p.mark(("app" + tag, idx), "sV")

            @block.scalar
            def _(s, idx=idx):
                p.w(s, "sA")
                p.wm(s, ("app" + tag, idx))
                p.inc(s.activation(out=dst[:, dst_off:dst_off + width],
                                   in_=sB[:, so:so + width], func=AF.Copy,
                                   scale=gam_t), "sA")
                p.mark(("fin" + tag, idx), "sA")

        with nc.psum_tensor("psB0", [128, 2048], F32) as _b0, \
                nc.psum_tensor("psB1", [128, 2048], F32) as _b1:
            psb = [_b0.ap(), _b1.ap()]
            for kv in range(HK):
                if kv >= 1:
                    hb2b(kv - 1, 2048, gamk, K, (kv - 1) * 2048, "K", 0)
                hb1(kv, 2048, wkTd, "K", psb[kv % 2], psb[1 - kv % 2], 0, 1)
                hb2a(kv, 2048, cosk, sink, "K",
                     psb[kv % 2], psb[1 - kv % 2], 0)

                @block.sync
                def _(sync):
                    scale_loads(sync, 2)
            hb2b(HK - 1, 2048, gamk, K, (HK - 1) * 2048, "K", 0)

        if upto < 3.2:
            raise _Stop

        @block.sync
        def _(sync):
            p.wm(sync, ("roK", HK - 1))
            for dst, srcd in [(cosq, cosqd), (sinq, sinqd)]:
                p.inc(sync.dma_start(out=dst, in_=srcd), "rtq", 16)
            p.mark("ropetabq", "rtq")

        with nc.psum_tensor("psQ0", [128, 1024], F32) as _q0, \
                nc.psum_tensor("psQ1", [128, 1024], F32) as _q1, \
                nc.psum_tensor("psQ2", [128, 1024], F32) as _q2, \
                nc.psum_tensor("psQ3", [128, 1024], F32) as _q3:
            psq = [_q0.ap(), _q1.ap(), _q2.ap(), _q3.ap()]

            def qargs(h):
                pq = h % 2
                return (psq[pq * 2], psq[pq * 2 + 1], pq * 1024)

            hb1(0, 1024, wqTd, "Q", *qargs(0), 2)
            for h in range(1, H):
                if h >= 2:
                    hb2b(h - 2, 1024, gamq, Q, (h - 2) * 1024, "Q",
                         (h % 2) * 1024)
                pp, ppo, so = qargs(h)
                hb1(h, 1024, wqTd, "Q", pp, ppo, so, 2)
                ppp, pppo, pso = qargs(h - 1)
                hb2a(h - 1, 1024, cosq, sinq, "Q", ppp, pppo, pso)

                @block.sync
                def _(sync):
                    scale_loads(sync, 2)
            hb2a(H - 1, 1024, cosq, sinq, "Q", *qargs(H - 1))
            hb2b(H - 2, 1024, gamq, Q, (H - 2) * 1024, "Q",
                 ((H - 2) % 2) * 1024)
            hb2b(H - 1, 1024, gamq, Q, (H - 1) * 1024, "Q",
                 ((H - 1) % 2) * 1024)

        p.mark("B2end_sP", "sP")
        p.mark("B2end_sV", "sV")
        p.mark("B2end_sA", "sA")

        if upto < 4:
            raise _Stop
        with nc.psum_tensor("psST0", [128, 1152], F32) as _s0, \
                nc.psum_tensor("psST1", [128, 1152], F32) as _s1, \
                nc.psum_tensor("psO0", [128, 128], F32) as _o0, \
                nc.psum_tensor("psO1", [128, 128], F32) as _o1:
            psST = [_s0.ap(), _s1.ap()]
            psO = [_o0.ap(), _o1.ap()]

            @block.sync
            def _(sync):
                p.wm(sync, "B2end_sA")
                p.wm(sync, "B2end_sV")
                p.wm(sync, "B2end_sP")
                p.inc(sync.dma_start(out=masks, in_=m01d), "mskl", 16)
                p.mark("maskld", "mskl")

            NSTR = H * NQT

            def emit_st(t, n):
                h, i = n // 8, n % 8
                kv = h // 4
                j = n % 2
                if n == 0:
                    p.wm(t, "B2end_sA")
                if n >= 2:
                    p.wm(t, ("Cdencp", n - 2))
                    if (n - 2) % 8 == 7:
                        p.wm(t, ("Cnormrd", (n - 2) // 8))
                inst = None
                for s_ in range(NSLOT):
                    mt = i + s_
                    inst = t.matmul(
                        out=psST[j][:, s_ * 128:(s_ + 1) * 128],
                        lhsT=K[:, kv * 2048 + mt * 128:kv * 2048 + (mt + 1) * 128],
                        rhs=Q[:, h * 1024 + i * 128:h * 1024 + (i + 1) * 128],
                        start=True, stop=True)
                p.inc(inst, "sP")
                p.mark(("Cst", n), "sP")

            @block.tensor
            def _(t):
                emit_st(t, 0)

            for n in range(NSTR):
                h, i = n // 8, n % 8
                kv = h // 4
                j = n % 2

                @block.tensor
                def _(t, n=n):
                    if n + 1 < NSTR:
                        emit_st(t, n + 1)

                @block.scalar
                def _(s, j=j, n=n):
                    p.w(s, "sA")
                    p.wm(s, ("Cst", n))
                    if n == 0:
                        p.wm(s, "maskld")
                    if n >= 2:
                        p.wm(s, ("Cptm", n - 2))
                    p.inc(s.activation(out=ptraw[j], in_=psST[j],
                                       func=AF.Exp, scale=SCALE,
                                       bias=biasB), "sA")
                    p.mark(("Cexp", n), "sA")

                @block.sync
                def _(sync):
                    scale_loads(sync, 1)

                @block.vector
                def _(v, i=i, j=j, n=n):
                    p.w(v, "sV")
                    p.wm(v, ("Cexp", n))
                    if n >= 2:
                        p.wm(v, ("Cpv", n - 2))
                    p.inc(v.tensor_tensor(
                        out=ptm[j], in0=ptraw[j],
                        in1=masks[:, i * 1152:(i + 1) * 1152],
                        op=ALU.mult), "sV")
                    p.mark(("Cptm", n), "sV")
                    scale_dve(v, 1)

                @block.tensor
                def _(t, j=j, n=n, kv=kv, i=i):
                    p.wm(t, ("Cptm", n))
                    inst = None
                    for s_ in range(NSLOT):
                        inst = t.matmul(
                            out=psST[j][0:1, 0:128], lhsT=onesc,
                            rhs=ptm[j][:, s_ * 128:(s_ + 1) * 128],
                            start=(s_ == 0), stop=(s_ == NSLOT - 1))
                    p.inc(inst, "sP")
                    p.mark(("Cden", n), "sP")
                    if n >= 2:
                        p.wm(t, ("Coe", n - 2))
                    for s_ in range(NSLOT):
                        mt = i + s_
                        inst = t.matmul(
                            out=psO[j],
                            lhsT=V[:, mt * 512 + kv * 128:
                                   mt * 512 + (kv + 1) * 128],
                            rhs=ptm[j][:, s_ * 128:(s_ + 1) * 128],
                            start=(s_ == 0), stop=(s_ == NSLOT - 1))
                    p.inc(inst, "sP")
                    p.mark(("Cpv", n), "sP")

                @block.vector
                def _(v, j=j, n=n, i=i):
                    p.w(v, "sV")
                    p.wm(v, ("Cden", n))
                    p.inc(v.tensor_copy(out=den_sb[0:1, i * 128:(i + 1) * 128],
                                        in_=psST[j][0:1, 0:128]), "sV")
                    p.mark(("Cdencp", n), "sV")

                @block.vector
                def _(v, j=j, n=n, h=h, i=i):
                    p.w(v, "sV")
                    p.wm(v, ("Cpv", n))
                    p.inc(v.tensor_copy(
                        out=attnout[:, h * 1024 + i * 128:
                                    h * 1024 + (i + 1) * 128],
                        in_=psO[j]), "sV")
                    p.mark(("Coe", n), "sV")

                if i == 7:
                    jb = n % 2

                    @block.vector
                    def _(v, n=n):
                        p.w(v, "sV")
                        if n // 8 >= 1:
                            p.wm(v, ("Cbc", n // 8 - 1))
                        p.inc(v.reciprocal(out=rden_sb[0:1, 0:1024],
                                           in_=den_sb[0:1, 0:1024]), "sV")
                        p.mark(("Crecip", n // 8), "sV")

                    @block.tensor
                    def _(t, n=n, jb=jb):
                        p.wm(t, ("Crecip", n // 8))
                        inst = None
                        for qq in range(2):
                            inst = t.matmul(
                                out=psST[jb][:, qq * 512:(qq + 1) * 512],
                                lhsT=onesrf[0:1, :],
                                rhs=rden_sb[0:1, qq * 512:(qq + 1) * 512],
                                start=True, stop=True)
                        p.inc(inst, "sP")
                        p.mark(("Cbc", n // 8), "sP")

                    @block.vector
                    def _(v, n=n, h=h, jb=jb):
                        p.w(v, "sV")
                        p.wm(v, ("Cbc", n // 8))
                        for k2 in range(8):
                            p.wm(v, ("Coe", n - 7 + k2))
                        sl = attnout[:, h * 1024:(h + 1) * 1024]
                        p.inc(v.tensor_tensor(out=sl, in0=sl,
                                              in1=psST[jb][:, 0:1024],
                                              op=ALU.mult), "sV")
                        p.mark(("Cnormrd", h), "sV")

            p.mark("Cend_sP", "sP")
            p.mark("Cend_sV", "sV")
            p.mark("Cend_sA", "sA")

        if upto < 5:
            raise _Stop
        with nc.psum_tensor("psY0", [128, 2048], F32) as _y0, \
                nc.psum_tensor("psY1", [128, 2048], F32) as _y1:
            psY = [_y0.ap(), _y1.ap()]

            def emit_tr(t, g):
                for jj in (2 * g, 2 * g + 1):
                    jv = jj % 2
                    p.wm(t, ("Dl2", jj))
                    if g < 3:
                        p.wm(t, ("Dycopy", 2 * g + 2))
                        p.wm(t, ("Dycopy", 2 * g + 3))
                    inst = None
                    for c in range(16):
                        inst = t.matmul(
                            out=psY[jv][:, c * 128:(c + 1) * 128],
                            lhsT=xl2tm[jv][:, c * 128:(c + 1) * 128],
                            rhs=ident, start=True, stop=True)
                    p.inc(inst, "sP")
                    p.mark(("Dl2tr", jj), "sP")

            def emit_trevac(v, g):
                for jj in (2 * g, 2 * g + 1):
                    jv = jj % 2
                    p.w(v, "sV")
                    p.wm(v, ("Dl2tr", jj))
                    dst = xln2T.rearrange("p (c n) -> p c n", c=16)[
                        :, :, jj * 128:(jj + 1) * 128]
                    srcv = psY[jv].rearrange("p (c n) -> p c n", c=16)
                    p.inc(v.tensor_copy(out=dst, in_=srcv), "sV")
                    p.mark(("Dtrevac", jj), "sV")

            for grp in range(4):
                j0, j1 = 2 * grp, 2 * grp + 1

                @block.sync
                def _(sync, grp=grp, j0=j0, j1=j1):
                    if grp == 0:
                        p.wm(sync, "Cend_sV")
                        p.wm(sync, "Cend_sA")
                        p.wm(sync, "Cend_sP")
                    for jj, buf in ((j0, xr[0]), (j1, xr[1])):
                        if grp >= 1:
                            p.wm(sync, ("Dx2st", jj - 2))
                        p.inc(sync.dma_start(
                            out=buf, in_=xd[1024 + jj * 128:
                                            1024 + (jj + 1) * 128, :]),
                            f"xrl{jj % 2}", 16)
                        p.mark(("Dxld", jj), f"xrl{jj % 2}")
                    scale_loads(sync, 4)

                for ec in range(16):
                    gec = grp * 16 + ec

                    @block.sync
                    def _(sync, grp=grp, ec=ec, gec=gec):
                        if gec == 0:
                            p.wm(sync, "Cend_sV")
                            p.wm(sync, "Cend_sP")
                            p.wm(sync, "Cend_sA")
                            p.wm(sync, ("cv_wo", 15))
                        if gec >= 2:
                            p.wm(sync, ("Dwous", gec - 2))
                        srcw = wobf[ec * 128:(ec + 1) * 128, :]
                        p.inc(sync.dma_start(out=wotb[ec % 2], in_=srcw),
                              f"wotl{ec % 2}", 16)
                        p.mark(("Dwcast", gec), f"wotl{ec % 2}")
                        p.mark(("Dwold", gec), f"wotl{ec % 2}")
                        scale_loads(sync, 1)

                    @block.vector
                    def _(v):
                        scale_dve(v, 1)

                    @block.tensor
                    def _(t, grp=grp, ec=ec, j0=j0, j1=j1, gec=gec):
                        p.wm(t, ("Dwcast", gec))
                        if ec == 0:
                            if grp >= 1:
                                p.wm(t, ("Dycopy", j0 - 2))
                                p.wm(t, ("Dycopy", j1 - 2))
                            if grp >= 2:
                                p.wm(t, ("Dtrevac", j0 - 4))
                                p.wm(t, ("Dtrevac", j1 - 4))
                        inst = None
                        for jj, pps in ((j0, psY[0]), (j1, psY[1])):
                            lo = ec * 1024 + jj * 128
                            for qq in range(4):
                                inst = t.matmul(
                                    out=pps[:, qq * 512:(qq + 1) * 512],
                                    lhsT=attnout[:, lo:lo + 128],
                                    rhs=wotb[ec % 2][:, qq * 512:(qq + 1) * 512],
                                    start=(ec == 0), stop=(ec == 15))
                        p.inc(inst, "sP")
                        p.mark(("Dwous", gec), "sP")
                        if ec == 15:
                            p.mark(("Dyp", j0), "sP")
                            p.mark(("Dyp", j1), "sP")

                for jj in (j0, j1):
                    jv = jj % 2

                    @block.vector
                    def _(v, jj=jj, jv=jv):
                        p.w(v, "sV")
                        p.wm(v, ("Dyp", jj))
                        if jj >= 2:
                            p.wm(v, ("Dl2", jj - 2))
                        p.inc(v.tensor_copy(out=ybuf[jv], in_=psY[jv]), "sV")
                        p.mark(("Dycopy", jj), "sV")

                if grp >= 1:
                    @block.tensor
                    def _(t, grp=grp):
                        emit_tr(t, grp - 1)

                    @block.vector
                    def _(v, grp=grp):
                        emit_trevac(v, grp - 1)

                for jj in (j0, j1):
                    jv = jj % 2

                    @block.vector
                    def _(v, jj=jj, jv=jv):
                        p.w(v, "sV")
                        p.wm(v, ("Dycopy", jj))
                        ln_stats(v, ybuf[jv], mvv[jv])
                        p.mark(("Dmv", jj), "sV")

                    @block.scalar
                    def _(s, jj=jj, jv=jv):
                        p.w(s, "sA")
                        p.wm(s, ("Dmv", jj))
                        p.inc(s.activation(out=stdv[:, jv:jv + 1],
                                           in_=mvv[jv][:, 1:2], func=AF.Sqrt,
                                           bias=epsln), "sA")
                        p.mark(("Dstd", jj), "sA")

                    @block.vector
                    def _(v, jj=jj, jv=jv):
                        p.w(v, "sV")
                        p.wm(v, ("Dstd", jj))
                        p.inc(v.reciprocal(out=rstdv[:, jv:jv + 1],
                                           in_=stdv[:, jv:jv + 1]), "sV")
                        p.w(v, "sV")
                        p.inc(v.tensor_scalar(out=ytile[jv], in0=ybuf[jv],
                                              scalar1=mvv[jv][:, 0:1],
                                              scalar2=rstdv[:, jv:jv + 1],
                                              op0=ALU.subtract, op1=ALU.mult),
                              "sV")
                        p.mark(("Dyt", jj), "sV")
                        p.wm(v, ("Dxld", jj))
                        if jj >= 2:
                            p.wm(v, ("Dx2st", jj - 2))
                        p.w(v, "sV")
                        p.inc(v.tensor_tensor(out=x2t[jv], in0=ytile[jv],
                                              in1=xr[jv], op=ALU.add), "sV")
                        p.mark(("Dx2", jj), "sV")
                        ln_stats(v, x2t[jv], mvv[jv])
                        p.mark(("Dmv2", jj), "sV")

                    @block.sync
                    def _(sync, jj=jj, jv=jv):
                        p.wm(sync, ("Dx2", jj))
                        p.inc(sync.dma_start(
                            out=x2scr[jj * 128:(jj + 1) * 128, :],
                            in_=x2t[jv]), f"x2s{jv}", 16)
                        p.mark(("Dx2st", jj), f"x2s{jv}")

                    @block.scalar
                    def _(s, jj=jj, jv=jv):
                        p.w(s, "sA")
                        p.wm(s, ("Dmv2", jj))
                        p.inc(s.activation(out=stdv[:, jv:jv + 1],
                                           in_=mvv[jv][:, 1:2], func=AF.Sqrt,
                                           bias=epsln), "sA")
                        p.mark(("Dstd2", jj), "sA")

                    @block.vector
                    def _(v, jj=jj, jv=jv):
                        p.w(v, "sV")
                        p.wm(v, ("Dstd2", jj))
                        p.inc(v.reciprocal(out=rstdv[:, jv:jv + 1],
                                           in_=stdv[:, jv:jv + 1]), "sV")
                        p.w(v, "sV")
                        if jj >= 2:
                            p.wm(v, ("Dl2tr", jj - 2))
                        p.inc(v.tensor_scalar(out=xl2tm[jv], in0=x2t[jv],
                                              scalar1=mvv[jv][:, 0:1],
                                              scalar2=rstdv[:, jv:jv + 1],
                                              op0=ALU.subtract, op1=ALU.mult),
                              "sV")
                        p.mark(("Dl2", jj), "sV")
                        scale_dve(v, 1)

            @block.tensor
            def _(t):
                emit_tr(t, 3)

            @block.vector
            def _(v):
                emit_trevac(v, 3)

            while sc["ld"] < NSC2 or sc["act"] < NSC2:
                @block.sync
                def _(sync):
                    scale_loads(sync, 4)

                @block.vector
                def _(v):
                    scale_dve(v, 4)

            @block.vector
            def _(v):
                p.w(v, "sV")
                p.w(v, "sA")
                p.inc(v.reduce_sum(out=tiny[:, 3:4], in_=sstr1, axis=AX.X),
                      "sV")
                p.mark(("th1red", 0), "sV")

            @block.tensor
            def _(t):
                p.wm(t, ("th1red", 0))
                p.wm(t, ("Dtrevac", 6))
                p.inc(t.matmul(out=psY[0][0:1, 4:5], lhsT=onescf,
                               rhs=tiny[:, 3:4], start=True, stop=True), "sP")
                p.mark(("th1mm", 0), "sP")

            @block.vector
            def _(v):
                p.w(v, "sV")
                p.wm(v, ("th1mm", 0))
                p.inc(v.tensor_copy(out=trow[0:1, 3:4],
                                    in_=psY[0][0:1, 4:5]), "sV")
                p.mark(("th1row", 0), "sV")

            @block.tensor
            def _(t):
                p.wm(t, ("th1row", 0))
                p.inc(t.matmul(out=psY[0][:, 256:257], lhsT=onesrf[0:1, :],
                               rhs=trow[0:1, 3:4], start=True, stop=True),
                      "sP")
                p.mark(("th1bc", 0), "sP")

            @block.vector
            def _(v):
                p.w(v, "sV")
                p.wm(v, ("th1bc", 0))
                _c = 0.7 / float(DFF * DM)
                _hi = float(np.float32(_c))
                _lo = float(np.float32(_c - _hi))
                p.inc(v.tensor_scalar(out=theta1, in0=psY[0][:, 256:257],
                                      scalar1=_hi, scalar2=None,
                                      op0=ALU.mult), "sV")
                p.w(v, "sV")
                p.inc(v.scalar_tensor_tensor(out=theta1,
                                             in0=psY[0][:, 256:257],
                                             scalar=_lo, in1=theta1,
                                             op0=ALU.mult, op1=ALU.add),
                      "sV")
                p.w(v, "sV")
                p.inc(v.tensor_scalar(out=negth1, in0=theta1, scalar1=-1.0,
                                      scalar2=None, op0=ALU.mult), "sV")
                p.mark(("th1done", 0), "sV")

            @block.vector
            def _(v):
                p.w(v, "sV")
                p.inc(v.reduce_sum(out=tiny[:, 4:5], in_=sstr2, axis=AX.X),
                      "sV")
                p.mark(("th2red", 0), "sV")

            @block.tensor
            def _(t):
                p.wm(t, ("th2red", 0))
                p.inc(t.matmul(out=psY[0][0:1, 0:1], lhsT=onescf,
                               rhs=tiny[:, 4:5], start=True, stop=True), "sP")
                p.mark(("th2mm", 0), "sP")

            @block.vector
            def _(v):
                p.w(v, "sV")
                p.wm(v, ("th2mm", 0))
                p.inc(v.tensor_copy(out=trow[0:1, 4:5],
                                    in_=psY[0][0:1, 0:1]), "sV")
                p.mark(("th2row", 0), "sV")

            @block.tensor
            def _(t):
                p.wm(t, ("th2row", 0))
                p.inc(t.matmul(out=psY[0][:, 512:513], lhsT=onesrf[0:1, :],
                               rhs=trow[0:1, 4:5], start=True, stop=True),
                      "sP")
                p.mark(("th2bc", 0), "sP")

            @block.vector
            def _(v):
                p.w(v, "sV")
                p.wm(v, ("th2bc", 0))
                _c = 0.7 / float(DFF * DM)
                _hi = float(np.float32(_c))
                _lo = float(np.float32(_c - _hi))
                p.inc(v.tensor_scalar(out=theta2, in0=psY[0][:, 512:513],
                                      scalar1=_hi, scalar2=None,
                                      op0=ALU.mult), "sV")
                p.w(v, "sV")
                p.inc(v.scalar_tensor_tensor(out=theta2,
                                             in0=psY[0][:, 512:513],
                                             scalar=_lo, in1=theta2,
                                             op0=ALU.mult, op1=ALU.add),
                      "sV")
                p.w(v, "sV")
                p.inc(v.tensor_scalar(out=negth2, in0=theta2, scalar1=-1.0,
                                      scalar2=None, op0=ALU.mult), "sV")
                p.mark(("th2done", 0), "sV")

            p.mark("Dend_sP", "sP")
            p.mark("Dend_sV", "sV")
            p.mark("Dend_sA", "sA")
            p.mark("Dend_sPo", "sPo")

        if upto < 6:
            raise _Stop
        with nc.psum_tensor("psH", [128, 1024], F32) as _h0, \
                nc.psum_tensor("psOU0", [128, 1024], F32) as _u0, \
                nc.psum_tensor("psOU1", [128, 1024], F32) as _u1:
            psHq = [_h0.ap()[:, 0:512], _h0.ap()[:, 512:1024]]
            psOU = [_u0.ap(), _u1.ap()]

            @block.vector
            def _(v):
                p.w(v, "sV")
                p.wm(v, "Dend_sV")
                inst = None
                for dc in range(16):
                    inst = v.tensor_scalar(
                        out=out_acc[:, dc * 1024:(dc + 1) * 1024],
                        in0=out_acc[:, dc * 1024:(dc + 1) * 1024],
                        scalar1=0.0, scalar2=b2sb[:, dc:dc + 1],
                        op0=ALU.mult, op1=ALU.add)
                p.inc(inst, "sV")
                p.mark("Einit", "sV")

            for fc in range(8):
                for k in range(8):
                    fs = fc * 8 + k
                    jj = fs % 2

                    @block.sync
                    def _(sync, fs=fs, jj=jj):
                        if fs == 0:
                            p.wm(sync, "Dend_sP")
                            p.wm(sync, "Dend_sV")
                            p.wm(sync, "Dend_sA")
                            p.wm(sync, "Dend_sPo")
                            p.w(sync, "x2s0")
                            p.w(sync, "x2s1")
                        if fs >= 2:
                            p.wm(sync, ("Estt1", fs - 2))
                        p.inc(sync.dma_start(
                            out=w1b[jj], in_=w1pd[fs * 128:(fs + 1) * 128, :]),
                            f"w1l{jj}", 16)
                        p.mark(("Ew1ld", fs), f"w1l{jj}")
                        if fs >= 2:
                            p.wm(sync, ("Estt2", fs - 2))
                        p.inc(sync.dma_start(
                            out=w2b[jj], in_=w2Td[fs * 128:(fs + 1) * 128, :]),
                            f"w2l{jj}", 16)
                        p.mark(("Ew2ld", fs), f"w2l{jj}")

                    @block.vector
                    def _(v, fs=fs, jj=jj):
                        p.w(v, "sV")
                        p.wm(v, ("Ew1ld", fs))
                        if fs == 0:
                            p.wm(v, ("th1done", 0))
                        if fs >= 2:
                            p.wm(v, ("Ehmm", fs - 2, 1))
                        p.inc(v.tensor_scalar(out=a1m, in0=w1b[jj],
                                              scalar1=theta1, scalar2=None,
                                              op0=ALU.is_ge), "sV")
                        p.w(v, "sV")
                        p.inc(v.scalar_tensor_tensor(
                            out=w1qn[jj], in0=w1b[jj], scalar=negth1,
                            in1=a1m, op0=ALU.is_le, op1=ALU.subtract), "sV")
                        p.mark(("Estt1", fs), "sV")

                    @block.vector
                    def _(v, fs=fs, jj=jj, fc=fc):
                        p.w(v, "sV")
                        p.wm(v, ("Ew2ld", fs))
                        if fs == 0:
                            p.wm(v, ("th2done", 0))
                        if fc >= 1 and fs % 8 == 0:
                            p.wm(v, ("Eoumm", (fc - 1) * 16 + 15))
                        p.inc(v.tensor_scalar(out=a2m, in0=w2b[jj],
                                              scalar1=theta2, scalar2=None,
                                              op0=ALU.is_ge), "sV")
                        p.w(v, "sV")
                        p.inc(v.scalar_tensor_tensor(
                            out=w2qn[fs % 8], in0=w2b[jj], scalar=negth2,
                            in1=a2m, op0=ALU.is_le, op1=ALU.subtract), "sV")
                        p.mark(("Estt2", fs), "sV")

                    @block.tensor
                    def _(t, fs=fs, jj=jj):
                        p.wm(t, ("Estt1", fs))
                        for qq in range(2):
                            if fs >= 1:
                                p.wm(t, ("Egelu", fs - 1, qq))
                            inst = None
                            for dc in range(16):
                                inst = t.matmul(
                                    out=psHq[qq],
                                    lhsT=w1qn[jj][:, dc * 128:(dc + 1) * 128],
                                    rhs=xln2T[:, dc * 1024 + qq * 512:
                                              dc * 1024 + (qq + 1) * 512],
                                    start=(dc == 0), stop=(dc == 15))
                            p.inc(inst, "sP")
                            p.mark(("Ehmm", fs, qq), "sP")

                    @block.scalar
                    def _(s, fs=fs, k=k, fc=fc):
                        for qq in range(2):
                            p.w(s, "sA")
                            p.wm(s, ("Ehmm", fs, qq))
                            if fc >= 1 and k == 0 and qq == 0:
                                p.wm(s, ("Eoumm", (fc - 1) * 16 + 15))
                            p.inc(s.activation(
                                out=h_buf[:, k * 1024 + qq * 512:
                                          k * 1024 + (qq + 1) * 512],
                                in_=psHq[qq], func=AF.Gelu, scale=-1.0,
                                bias=b1sb[:, fs:fs + 1]), "sA")
                            p.mark(("Egelu", fs, qq), "sA")

                for dc in range(16):
                    ju = dc % 2

                    @block.tensor
                    def _(t, fc=fc, dc=dc, ju=ju):
                        if dc == 0:
                            p.wm(t, ("Egelu", fc * 8 + 7, 1))
                            for k2 in range(8):
                                p.wm(t, ("Estt2", fc * 8 + k2))
                        if fc * 16 + dc >= 2:
                            p.wm(t, ("Eacc", fc * 16 + dc - 2))
                        inst = None
                        for qq in range(2):
                            for k2 in range(8):
                                inst = t.matmul(
                                    out=psOU[ju][:, qq * 512:(qq + 1) * 512],
                                    lhsT=w2qn[k2][:, dc * 128:(dc + 1) * 128],
                                    rhs=h_buf[:, k2 * 1024 + qq * 512:
                                              k2 * 1024 + (qq + 1) * 512],
                                    start=(k2 == 0), stop=(k2 == 7))
                        p.inc(inst, "sP")
                        p.mark(("Eoumm", fc * 16 + dc), "sP")

                    @block.vector
                    def _(v, fc=fc, dc=dc, ju=ju):
                        p.w(v, "sV")
                        p.wm(v, ("Eoumm", fc * 16 + dc))
                        if fc == 0:
                            p.wm(v, "Einit")
                        acc = out_acc[:, dc * 1024:(dc + 1) * 1024]
                        p.inc(v.tensor_tensor(out=acc, in0=acc, in1=psOU[ju],
                                              op=ALU.subtract), "sV")
                        p.mark(("Eacc", fc * 16 + dc), "sV")

            p.mark("Eend_sP", "sP")
            p.mark("Eend_sV", "sV")
            p.mark("Eend_sA", "sA")

        if upto < 7:
            raise _Stop
        with nc.psum_tensor("psF0", [128, 2048], F32) as _f0, \
                nc.psum_tensor("psF1", [128, 2048], F32) as _f1:
            psF = [_f0.ap(), _f1.ap()]
            for jj in range(NQT):
                jv = jj % 2

                @block.sync
                def _(sync, jj=jj, jv=jv):
                    if jj == 0:
                        p.wm(sync, "Eend_sV")
                        p.w(sync, "x2s0")
                        p.w(sync, "x2s1")
                    if jj >= 2:
                        p.wm(sync, ("Fout", jj - 2))
                    p.inc(sync.dma_start(
                        out=x2b[jv], in_=x2scr[jj * 128:(jj + 1) * 128, :]),
                        f"x2l{jv}", 16)
                    p.mark(("Fx2ld", jj), f"x2l{jv}")

                @block.scalar
                def _(s, jj=jj, jv=jv):
                    p.w(s, "sA")
                    if jj == 0:
                        p.wm(s, "Eend_sV")
                    if jj >= 2:
                        p.wm(s, ("Ftr", jj - 2))
                    src = out_acc.rearrange("p (c n) -> p c n", c=16)[
                        :, :, jj * 128:(jj + 1) * 128]
                    dst = acb[jv].rearrange("p (c n) -> p c n", c=16)
                    p.inc(s.activation(out=dst, in_=src, func=AF.Copy), "sA")
                    p.mark(("Fcast", jj), "sA")

                @block.tensor
                def _(t, jj=jj, jv=jv):
                    p.wm(t, ("Fcast", jj))
                    if jj == 0:
                        p.wm(t, "Eend_sV")
                    if jj >= 2:
                        p.wm(t, ("Fapplied", jj - 2))
                    inst = None
                    for c in range(16):
                        inst = t.matmul(
                            out=psF[jv][:, c * 128:(c + 1) * 128],
                            lhsT=acb[jv][:, c * 128:(c + 1) * 128],
                            rhs=ident, start=True, stop=True)
                    p.inc(inst, "sP")
                    p.mark(("Ftr", jj), "sP")

                @block.vector
                def _(v, jj=jj, jv=jv):
                    p.w(v, "sV")
                    p.wm(v, ("Ftr", jj))
                    ln_stats(v, psF[jv], mvv[jv])
                    p.mark(("Fmv", jj), "sV")

                @block.scalar
                def _(s, jj=jj, jv=jv):
                    p.w(s, "sA")
                    p.wm(s, ("Fmv", jj))
                    p.inc(s.activation(out=stdv[:, jv:jv + 1],
                                       in_=mvv[jv][:, 1:2], func=AF.Sqrt,
                                       bias=epsln), "sA")
                    p.mark(("Fstd", jj), "sA")

                @block.vector
                def _(v, jj=jj, jv=jv):
                    p.w(v, "sV")
                    p.wm(v, ("Fstd", jj))
                    p.inc(v.reciprocal(out=rstdv[:, jv:jv + 1],
                                       in_=stdv[:, jv:jv + 1]), "sV")
                    p.w(v, "sV")
                    if jj >= 2:
                        p.wm(v, ("Fstore", jj - 2))
                    p.inc(v.tensor_scalar(out=outb[jv], in0=psF[jv],
                                          scalar1=mvv[jv][:, 0:1],
                                          scalar2=rstdv[:, jv:jv + 1],
                                          op0=ALU.subtract, op1=ALU.mult),
                          "sV")
                    p.mark(("Fapplied", jj), "sV")
                    p.wm(v, ("Fx2ld", jj))
                    p.w(v, "sV")
                    p.inc(v.tensor_tensor(out=outb[jv], in0=outb[jv],
                                          in1=x2b[jv], op=ALU.add), "sV")
                    p.mark(("Fout", jj), "sV")

                @block.sync
                def _(sync, jj=jj, jv=jv):
                    p.wm(sync, ("Fout", jj))
                    p.inc(sync.dma_start(
                        out=outd[jj * 128:(jj + 1) * 128, :], in_=outb[jv]),
                        f"outs{jv}", 16)
                    p.mark(("Fstore", jj), f"outs{jv}")

            @block.sync
            def _(sync):
                p.w(sync, "outs0")
                p.w(sync, "outs1")
    except _Stop:
        pass

    return nc


def _host_constants(half):
    f64 = np.float64
    inv = 1.0 / (10000.0 ** (np.arange(0, 128, 2, dtype=f64) / 128.0))

    def tables(pos):
        ang = pos[None, :] * inv[:, None]
        c = np.concatenate([np.cos(ang), np.cos(ang)], 0)
        s = np.concatenate([np.sin(ang), np.sin(ang)], 0)
        return (np.ascontiguousarray(c.astype(np.float32)),
                np.ascontiguousarray(s.astype(np.float32)))

    qpos = (half * 1024 + np.arange(TQ)).astype(f64)
    kpos = (((1 - half) * 1024 + np.arange(TKV)) % 2048).astype(f64)
    cosq, sinq = tables(qpos)
    cosk, sink = tables(kpos)

    m01 = np.zeros((128, NQT * NSLOT * 128), dtype=np.float32)
    kk = np.arange(128)
    for i in range(NQT):
        gq = half * 1024 + i * 128 + np.arange(128)
        for s in range(NSLOT):
            m = i + s
            gk = kpos[m * 128 + kk]
            diff = gq[None, :] - gk[:, None]
            m01[:, (i * NSLOT + s) * 128:(i * NSLOT + s + 1) * 128] = (
                (diff >= 0) & (diff <= 1024)).astype(np.float32)
    return cosq, sinq, cosk, sink, np.ascontiguousarray(
        m01.astype(ml_dtypes.bfloat16))


def make_in_maps(L0_attn_x, L0_attn_wq, L0_attn_wk, L0_attn_wv, L0_attn_wo,
                 L0_attn_gamma_q, L0_attn_gamma_k, L0_ffn_w1, L0_ffn_b1,
                 L0_ffn_w2, L0_ffn_b2):
    bf = ml_dtypes.bfloat16
    cc = np.ascontiguousarray
    shared = {
        "wqT": cc(np.asarray(L0_attn_wq, np.float32).T),
        "wkT": cc(np.asarray(L0_attn_wk, np.float32).T),
        "wvT": cc(np.asarray(L0_attn_wv, np.float32).T),
        "woT": cc(np.asarray(L0_attn_wo, np.float32).T),
        "w1": cc(np.asarray(L0_ffn_w1, np.float32)),
        "w1p": cc(np.asarray(L0_ffn_w1, np.float32)
                  .reshape(64, 128, 16, 128).transpose(0, 3, 2, 1)
                  .reshape(8192, 2048)),
        "w2T": cc(np.asarray(L0_ffn_w2, np.float32).T),
        "b1s": cc(np.asarray(L0_ffn_b1, np.float32).reshape(64, 128).T),
        "b2s": cc(np.asarray(L0_ffn_b2, np.float32).reshape(16, 128).T),
        "gamq": cc(np.asarray(L0_attn_gamma_q, np.float32).reshape(128, 1)),
        "gamk": cc(np.asarray(L0_attn_gamma_k, np.float32).reshape(128, 1)),
        "gamqr": cc(np.asarray(L0_attn_gamma_q, np.float32).reshape(1, 128)),
        "gamkr": cc(np.asarray(L0_attn_gamma_k, np.float32).reshape(1, 128)),
        "ident": np.eye(128, dtype=bf),
        "onesm": np.ones((128, 128), dtype=bf),
        "onesc": np.ones((128, 1), dtype=bf),
        "onescf": np.ones((128, 1), dtype=np.float32),
        "onesrf": np.ones((1, 128), dtype=np.float32),
    }
    rtm = np.zeros((128, 128), dtype=np.float32)
    for kidx in range(64):
        rtm[kidx, kidx + 64] = 1.0
    for kidx in range(64, 128):
        rtm[kidx, kidx - 64] = -1.0
    shared["rt"] = rtm.astype(bf)

    halfconst = {}
    for half in range(2):
        cosq, sinq, cosk, sink, m01 = _host_constants(half)
        halfconst[half] = {"cosq": cosq, "sinq": sinq, "cosk": cosk,
                           "sink": sink, "m01": m01}

    x = np.asarray(L0_attn_x, np.float32)
    in_maps = []
    for core in range(8):
        b, half = core // 2, core % 2
        d = dict(shared)
        d.update(halfconst[half])
        d["x"] = cc(np.roll(x[b], -(1 - half) * 1024, axis=0))
        in_maps.append(d)
    return in_maps


def kernel(**ins):
    in_maps = make_in_maps(**ins)
    nc = build_program()
    res = run_bass_kernel_spmd(nc, in_maps, core_ids=list(range(8)))
    out = np.empty((4, 2048, 2048), dtype=np.float32)
    for core in range(8):
        b, half = core // 2, core % 2
        out[b, half * 1024:(half + 1) * 1024, :] = res.results[core]["out"]
    return out


if __name__ == "__main__":
    import reference
    ins = reference.setup_inputs()
    ins = {k: np.asarray(v) for k, v in ins.items()}
    got = kernel(**ins)
    want = np.asarray(reference.reference(**ins))
    err = np.linalg.norm(got - want) / np.linalg.norm(want)
    print("rel_l2", err)


# version 19
# speedup vs baseline: 1.0107x; 1.0107x over previous
import math
from contextlib import ExitStack

import numpy as np
import ml_dtypes

import concourse.bass as bass
import concourse.mybir as mybir
from concourse.bass_utils import run_bass_kernel_spmd

F32 = mybir.dt.float32
BF16 = mybir.dt.bfloat16
AF = mybir.ActivationFunctionType
ALU = mybir.AluOpType
AX = mybir.AxisListType

H = 16
HK = 4
DM = 2048
DFF = 8192
TKV = 2048
TQ = 1024
NKT = 16
NQT = 8
NSLOT = 9
SCALE = 1.0 / math.sqrt(128.0)
EPS_LN = 1e-5
EPS_QK = 1e-6
PART_BYTES = 200 * 1024
NSC = 128


class P:
    def __init__(self, nc, ctx, names):
        self.nc = nc
        self.ctx = ctx
        self.s = {n: ctx.enter_context(nc.semaphore(n)) for n in names}
        self.c = {n: 0 for n in names}
        self.marks = {}

    def _ensure(self, name):
        if name not in self.s:
            self.s[name] = self.ctx.enter_context(self.nc.semaphore(name))
            self.c[name] = 0

    def inc(self, inst, name, k=1):
        self._ensure(name)
        self.c[name] += k
        inst.then_inc(self.s[name], k)
        return self.c[name]

    def w(self, eng, name, val=None):
        self._ensure(name)
        v = self.c[name] if val is None else val
        if v > 0:
            eng.wait_ge(self.s[name], v)

    def mark(self, key, name):
        self.marks[key] = (name, self.c[name])

    def wm(self, eng, key):
        name, v = self.marks[key]
        if v > 0:
            eng.wait_ge(self.s[name], v)


class Arena:

    def __init__(self, nc, ctx):
        self.t = ctx.enter_context(
            nc.sbuf_tensor("arena", [128, PART_BYTES], mybir.dt.uint8))
        self.claims = []

    def carve(self, name, cols, dtype, phases):
        esz = 4 if dtype == F32 else 2
        size = cols * esz
        phases = frozenset(phases)
        taken = sorted((o, s) for (o, s, ph, _n) in self.claims if ph & phases)
        off = 0
        for (o, s) in taken:
            if off + size <= o:
                break
            off = max(off, ((o + s + 63) // 64) * 64)
        assert off + size <= PART_BYTES, f"arena overflow at {name}"
        self.claims.append((off, size, phases, name))
        ap = self.t.ap().bitcast(dtype)
        return ap[:, off // esz: off // esz + cols]


class _Stop(Exception):
    pass


ARENA = {}


def build_program(upto=99):
    nc = bass.Bass("TRN2", target_bir_lowering=False)

    xd = nc.dram_tensor("x", [TKV, DM], F32, kind="ExternalInput").ap()
    wqTd = nc.dram_tensor("wqT", [DM, 2048], F32, kind="ExternalInput").ap()
    wkTd = nc.dram_tensor("wkT", [DM, 512], F32, kind="ExternalInput").ap()
    wvTd = nc.dram_tensor("wvT", [DM, 512], F32, kind="ExternalInput").ap()
    woTd = nc.dram_tensor("woT", [2048, DM], F32, kind="ExternalInput").ap()
    w1d = nc.dram_tensor("w1", [DFF, DM], F32, kind="ExternalInput").ap()
    w1pd = nc.dram_tensor("w1p", [DFF, DM], F32, kind="ExternalInput").ap()
    w2Td = nc.dram_tensor("w2T", [DFF, DM], F32, kind="ExternalInput").ap()
    b1sd = nc.dram_tensor("b1s", [128, 64], F32, kind="ExternalInput").ap()
    b2sd = nc.dram_tensor("b2s", [128, 16], F32, kind="ExternalInput").ap()
    gamqd = nc.dram_tensor("gamq", [128, 1], F32, kind="ExternalInput").ap()
    gamkd = nc.dram_tensor("gamk", [128, 1], F32, kind="ExternalInput").ap()
    gamqrd = nc.dram_tensor("gamqr", [1, 128], F32, kind="ExternalInput").ap()
    gamkrd = nc.dram_tensor("gamkr", [1, 128], F32, kind="ExternalInput").ap()
    identd = nc.dram_tensor("ident", [128, 128], BF16, kind="ExternalInput").ap()
    rtd = nc.dram_tensor("rt", [128, 128], BF16, kind="ExternalInput").ap()
    onesmd = nc.dram_tensor("onesm", [128, 128], BF16, kind="ExternalInput").ap()
    onescd = nc.dram_tensor("onesc", [128, 1], BF16, kind="ExternalInput").ap()
    onescfd = nc.dram_tensor("onescf", [128, 1], F32, kind="ExternalInput").ap()
    onesrfd = nc.dram_tensor("onesrf", [1, 128], F32, kind="ExternalInput").ap()
    cosqd = nc.dram_tensor("cosq", [128, TQ], F32, kind="ExternalInput").ap()
    sinqd = nc.dram_tensor("sinq", [128, TQ], F32, kind="ExternalInput").ap()
    coskd = nc.dram_tensor("cosk", [128, TKV], F32, kind="ExternalInput").ap()
    sinkd = nc.dram_tensor("sink", [128, TKV], F32, kind="ExternalInput").ap()
    m01d = nc.dram_tensor("m01", [128, NQT * NSLOT * 128], BF16,
                          kind="ExternalInput").ap()

    x2scr = nc.dram_tensor("x2scr", [TQ, DM], F32, kind="Internal").ap()
    wqbf = nc.dram_tensor("wqbf", [DM, 2048], BF16, kind="Internal").ap()
    wkbf = nc.dram_tensor("wkbf", [DM, 512], BF16, kind="Internal").ap()
    wvbf = nc.dram_tensor("wvbf", [DM, 512], BF16, kind="Internal").ap()
    wobf = nc.dram_tensor("wobf", [2048, DM], BF16, kind="Internal").ap()
    outd = nc.dram_tensor("out", [TQ, DM], F32, kind="ExternalOutput").ap()

    ctx = ExitStack()
    try:
      with ctx:
        ar = Arena(nc, ctx)
        SCP = {"A", "B1", "B2K", "B2Q", "C", "D"}
        B2 = {"B2K", "B2Q"}
        xln1T = ar.carve("xln1T", 32768, BF16, {"A", "B1"} | B2)
        out_acc = ar.carve("out_acc", 16384, F32, {"E", "F"})
        Q = ar.carve("Q", 16384, BF16, {"B2Q", "C"})
        K = ar.carve("K", 8192, BF16, B2 | {"C"})
        V = ar.carve("V", 8192, BF16, {"B1", "C"} | B2)
        attnout = ar.carve("attnout", 16384, BF16, {"C", "D"})
        xln2T = ar.carve("xln2T", 16384, BF16, {"D", "E"})
        masks = ar.carve("masks", 9216, BF16, {"C"})
        scw = [ar.carve(f"scw{j}", 1024, F32, SCP) for j in range(4)]
        scdump = ar.carve("scdump", 1024, F32, SCP)
        xbuf = [ar.carve(f"xbuf{j}", 2048, F32, {"A"}) for j in range(2)]
        wcf = [ar.carve(f"wcf{j}", 2048, F32, {"A"}) for j in range(2)]
        wcb = [ar.carve(f"wcb{j}", 2048, BF16, {"A"}) for j in range(2)]
        xlntm = [ar.carve(f"xlntm{j}", 2048, BF16, {"A"}) for j in range(2)]
        wvTs = ar.carve("wvTs", 8192, BF16, {"B1"})
        cosk = ar.carve("cosk", 2048, F32, {"B2K"})
        sink = ar.carve("sink", 2048, F32, {"B2K"})
        cosq = ar.carve("cosq", 1024, F32, {"B2Q"})
        sinq = ar.carve("sinq", 1024, F32, {"B2Q"})
        wst = [ar.carve(f"wst{j}", 2048, BF16, B2) for j in range(2)]
        sA = ar.carve("sA", 2048, F32, B2)
        sB = ar.carve("sB", 2048, F32, B2)
        sC = ar.carve("sC", 2048, F32, B2)
        sBF = ar.carve("sBF", 2048, BF16, B2)
        ptraw = [ar.carve(f"ptraw{j}", 1152, BF16, {"C"}) for j in range(2)]
        den_sb = ar.carve("den_sb", 1024, F32, {"C"})[0:1, :]
        rden_sb = ar.carve("rden_sb", 1024, F32, {"C"})[0:1, :]
        ptm = [ar.carve(f"ptm{j}", 1152, BF16, {"C"}) for j in range(2)]
        xr = [ar.carve(f"xr{j}", 2048, F32, {"D"}) for j in range(2)]
        ytile = [ar.carve(f"ytile{j}", 2048, F32, {"D"}) for j in range(2)]
        x2t = [ar.carve(f"x2t{j}", 2048, F32, {"D"}) for j in range(2)]
        xl2tm = [ar.carve(f"xl2tm{j}", 2048, BF16, {"D"}) for j in range(2)]
        wotb = [ar.carve(f"wotb{j}", 2048, BF16, {"D"}) for j in range(2)]
        ybuf = [ar.carve(f"ybuf{j}", 2048, F32, {"D"}) for j in range(2)]
        w1b = [ar.carve(f"w1b{j}", 2048, F32, {"E"}) for j in range(2)]
        w2b = [ar.carve(f"w2b{j}", 2048, F32, {"E"}) for j in range(2)]
        w1qn = [ar.carve(f"w1qn{j}", 2048, BF16, {"E"}) for j in range(2)]
        a1m = ar.carve("a1m", 2048, BF16, {"E"})
        a2m = ar.carve("a2m", 2048, BF16, {"E"})
        w2qn = [ar.carve(f"w2qn{j}", 2048, BF16, {"E"}) for j in range(8)]
        h_buf = ar.carve("h_buf", 8192, BF16, {"E"})
        x2b = [ar.carve(f"x2b{j}", 2048, F32, {"F"}) for j in range(2)]
        acb = [ar.carve(f"acb{j}", 2048, BF16, {"F"}) for j in range(2)]
        outb = [ar.carve(f"outb{j}", 2048, F32, {"F"}) for j in range(2)]
        ARENA.clear()
        for (o, s, ph, n) in ar.claims:
            ARENA[n] = (o, s)

        def sv(name, cols, dtype=F32):
            return ctx.enter_context(
                nc.sbuf_tensor(name, [128, cols], dtype)).ap()

        ident = sv("identsb", 128, BF16)
        rt = sv("rtsb", 128, BF16)
        onesm = sv("onesmsb", 128, BF16)
        onesc = sv("onescsb", 1, BF16)
        onescf = sv("onescfsb", 1)
        gamq = sv("gamqsb", 1)
        gamk = sv("gamksb", 1)
        b1sb = sv("b1sbs", 64)
        b2sb = sv("b2sbs", 16)
        biasB = sv("biasBs", 1)
        theta1 = sv("theta1s", 1)
        theta2 = sv("theta2s", 1)
        negth1 = sv("negth1s", 1)
        negth2 = sv("negth2s", 1)
        sstr1 = sv("sstr1s", NSC)
        sstr2 = sv("sstr2s", NSC)
        st6 = sv("st6s", 24)
        mvv = [sv(f"mv{j}s", 2) for j in range(2)]
        stdv = sv("stdvs", 2)
        rstdv = sv("rstdvs", 2)
        tiny = sv("tinys", 8)
        epsln = sv("epslns", 1)
        epsqk = sv("epsqks", 1)

        onesrf = ctx.enter_context(
            nc.sbuf_tensor("onesrfsb", [1, 128], F32)).ap()
        gqr = ctx.enter_context(nc.sbuf_tensor("gqrsb", [1, 128], F32)).ap()
        gkr = ctx.enter_context(nc.sbuf_tensor("gkrsb", [1, 128], F32)).ap()

        trow = ctx.enter_context(nc.sbuf_tensor("trowsb", [1, 8], F32)).ap()

        p = P(nc, ctx, ["sP", "sV", "sA", "sPo"])
        block = ctx.enter_context(nc.Block())

        w1flat = w1d.rearrange("a b -> (a b)").rearrange(
            "(c p n) -> c p n", p=128, n=1024)
        w2flat = w2Td.rearrange("a b -> (a b)").rearrange(
            "(c p n) -> c p n", p=128, n=1024)
        sc = {"ld": 0, "act": 0}
        NSC2 = 2 * NSC

        def scale_loads(sync, n):
            for _ in range(n):
                c = sc["ld"]
                if c >= NSC2:
                    return
                if c >= 4 and ("scA", c - 4) not in p.marks:
                    return
                sc["ld"] = c + 1
                if c >= 4:
                    p.wm(sync, ("scA", c - 4))
                srcp = w1flat[c] if c < NSC else w2flat[c - NSC]
                p.inc(sync.dma_start(out=scw[c % 4], in_=srcp),
                      f"scl{c % 4}", 16)
                p.mark(("scld", c), f"scl{c % 4}")

        def scale_pool(g, n):
            return

        def scale_act(s, n):
            return

        def scale_dve(v, n):
            for _ in range(n):
                c = sc["act"]
                if c >= sc["ld"]:
                    return
                sc["act"] = c + 1
                strip = sstr1 if c < NSC else sstr2
                p.wm(v, ("scld", c))
                p.w(v, "sV")
                p.inc(v.tensor_reduce(out=strip[:, c % NSC:c % NSC + 1],
                                      in_=scw[c % 4], axis=AX.X, op=ALU.add,
                                      apply_absolute_value=True), "sV")
                p.mark(("scA", c), "sV")


        def ln_stats(v, src_ap, mv):
            for q4 in range(4):
                p.w(v, "sV")
                p.inc(v.bn_stats(out=st6[:, q4 * 6:q4 * 6 + 6],
                                 in_=src_ap[:, q4 * 512:(q4 + 1) * 512]),
                      "sV")
            p.w(v, "sV")
            p.inc(v.bn_aggr(out=mv[:, 0:2], in_=st6[:, 0:24]), "sV")

        @block.sync
        def _(sync):
            for dst, src in [(ident, identd), (rt, rtd), (onesm, onesmd),
                             (onesc, onescd), (onescf, onescfd),
                             (gamq, gamqd), (gamk, gamkd),
                             (b1sb, b1sd), (b2sb, b2sd),
                             (gqr[0:1, :], gamqrd), (gkr[0:1, :], gamkrd),
                             (onesrf[0:1, :], onesrfd)]:
                p.inc(sync.dma_start(out=dst, in_=src), "cst", 16)
            p.mark("consts", "cst")

        with nc.psum_tensor("ps0", [128, 8], F32) as ps0t:
            ps0 = ps0t.ap()

            @block.vector
            def _(v):
                p.wm(v, "consts")
                p.inc(v.reduce_max(out=trow[0:1, 0:1], in_=gqr[0:1, :],
                                   axis=AX.X, apply_absolute_value=True), "sV")
                p.w(v, "sV")
                p.inc(v.reduce_max(out=trow[0:1, 1:2], in_=gkr[0:1, :],
                                   axis=AX.X, apply_absolute_value=True), "sV")
                p.w(v, "sV")
                p.inc(v.tensor_tensor(out=trow[0:1, 2:3], in0=trow[0:1, 0:1],
                                      in1=trow[0:1, 1:2], op=ALU.mult), "sV")
                p.w(v, "sV")
                p.inc(v.tensor_scalar(out=trow[0:1, 3:4], in0=trow[0:1, 2:3],
                                      scalar1=-SCALE, scalar2=None,
                                      op0=ALU.mult), "sV")
                p.mark("bBrow", "sV")

            @block.tensor
            def _(t):
                p.wm(t, "bBrow")
                p.inc(t.matmul(out=ps0[:, 0:1], lhsT=onesrf[0:1, :],
                               rhs=trow[0:1, 3:4], start=True, stop=True), "sP")
                p.mark("bBmm", "sP")

            @block.vector
            def _(v):
                p.wm(v, "bBmm")
                p.inc(v.tensor_copy(out=biasB, in_=ps0[:, 0:1]), "sV")
                p.mark("biasB", "sV")
                p.w(v, "sV")
                p.inc(v.memset(epsln, EPS_LN), "sV")
                p.mark("epsln", "sV")
                p.w(v, "sV")
                p.inc(v.memset(epsqk, EPS_QK), "sV")
                p.mark(("epsqk", 0), "sV")

        if upto < 1:
            raise _Stop
        cvt_jobs = []
        for c in range(16):
            cvt_jobs.append((woTd[c * 128:(c + 1) * 128, :],
                             wobf[c * 128:(c + 1) * 128, :], 2048, "wo", c))
        cvs = {"ld": 0, "cast": 0, "st": 0}

        def cvt_loads(sync, n):
            for _ in range(n):
                c = cvs["ld"]
                if c >= len(cvt_jobs):
                    return
                if c >= 2 and ("cvst", c - 2) not in p.marks:
                    return
                cvs["ld"] = c + 1
                srcv, dstv, w, tag, ci = cvt_jobs[c]
                if c >= 2:
                    p.wm(sync, ("cvst", c - 2))
                p.inc(sync.dma_start(out=wcf[c % 2][:, 0:w], in_=srcv),
                      f"wcl{c % 2}", 16)
                p.mark(("cvld", c), f"wcl{c % 2}")

        def cvt_casts(s, n):
            for _ in range(n):
                c = cvs["cast"]
                if c >= cvs["ld"]:
                    return
                cvs["cast"] = c + 1
                srcv, dstv, w, tag, ci = cvt_jobs[c]
                p.wm(s, ("cvld", c))
                p.w(s, "sA")
                p.inc(s.activation(out=wcb[c % 2][:, 0:w],
                                   in_=wcf[c % 2][:, 0:w], func=AF.Copy),
                      "sA")
                p.mark(("cvcast", c), "sA")

        def cvt_stores(sync, n):
            for _ in range(n):
                c = cvs["st"]
                if c >= cvs["cast"]:
                    return
                cvs["st"] = c + 1
                srcv, dstv, w, tag, ci = cvt_jobs[c]
                p.wm(sync, ("cvcast", c))
                p.inc(sync.dma_start(out=dstv, in_=wcb[c % 2][:, 0:w]),
                      f"wcs{c % 2}", 16)
                p.mark(("cvst", c), f"wcs{c % 2}")
                p.mark((f"cv_{tag}", ci), f"wcs{c % 2}")

        with nc.psum_tensor("psTrA0", [128, 2048], F32) as _a0, \
                nc.psum_tensor("psTrA1", [128, 2048], F32) as _a1:
            psTr = [_a0.ap(), _a1.ap()]
            for i in range(NKT):
                j = i % 2
                mv = mvv[j]

                @block.sync
                def _(sync, i=i, j=j):
                    if i >= 2:
                        p.wm(sync, ("Aapply", i - 2))
                    p.inc(sync.dma_start(out=xbuf[j],
                                         in_=xd[i * 128:(i + 1) * 128, :]),
                          f"xld{j}", 16)
                    p.mark(("Axld", i), f"xld{j}")
                    cvt_stores(sync, 1)
                    cvt_loads(sync, 1)
                    scale_loads(sync, 2)

                @block.vector
                def _(v, i=i, j=j, mv=mv):
                    p.w(v, "sV")
                    p.wm(v, ("Axld", i))
                    if i >= 2:
                        p.wm(v, ("Astd", i - 2))
                    ln_stats(v, xbuf[j], mv)
                    p.mark(("Amv", i), "sV")

                @block.scalar
                def _(s, i=i, j=j, mv=mv):
                    p.w(s, "sA")
                    p.wm(s, ("Amv", i))
                    p.inc(s.activation(out=stdv[:, j:j + 1], in_=mv[:, 1:2],
                                       func=AF.Sqrt, bias=epsln), "sA")
                    p.mark(("Astd", i), "sA")
                    cvt_casts(s, 1)

                @block.vector
                def _(v, i=i, j=j, mv=mv):
                    p.w(v, "sV")
                    p.wm(v, ("Astd", i))
                    p.inc(v.reciprocal(out=rstdv[:, j:j + 1],
                                       in_=stdv[:, j:j + 1]), "sV")
                    p.w(v, "sV")
                    if i >= 2:
                        p.wm(v, ("Atr", i - 2))
                    p.inc(v.tensor_scalar(out=xlntm[j], in0=xbuf[j],
                                          scalar1=mv[:, 0:1],
                                          scalar2=rstdv[:, j:j + 1],
                                          op0=ALU.subtract, op1=ALU.mult),
                          "sV")
                    p.mark(("Aapply", i), "sV")

                @block.tensor
                def _(t, i=i, j=j):
                    p.wm(t, ("Aapply", i))
                    if i >= 2:
                        p.wm(t, ("Aevac", i - 2))
                    inst = None
                    for c in range(16):
                        inst = t.matmul(
                            out=psTr[j][:, c * 128:(c + 1) * 128],
                            lhsT=xlntm[j][:, c * 128:(c + 1) * 128],
                            rhs=ident, start=True, stop=True)
                    p.inc(inst, "sP")
                    p.mark(("Atr", i), "sP")

                @block.vector
                def _(v, i=i, j=j):
                    p.w(v, "sV")
                    p.wm(v, ("Atr", i))
                    dst = xln1T.rearrange("p (c n) -> p c n", c=16)[
                        :, :, i * 128:(i + 1) * 128]
                    src = psTr[j].rearrange("p (c n) -> p c n", c=16)
                    p.inc(v.tensor_copy(out=dst, in_=src), "sV")
                    p.mark(("Aevac", i), "sV")
                    scale_dve(v, 2)

            while cvs["st"] < len(cvt_jobs):
                @block.sync
                def _(sync):
                    cvt_stores(sync, 2)
                    cvt_loads(sync, 2)

                @block.scalar
                def _(s):
                    cvt_casts(s, 2)

            p.mark("Aend_sV", "sV")
            p.mark("Aend_sP", "sP")
            p.mark("Aend_sA", "sA")

        if upto < 2:
            raise _Stop
        with nc.psum_tensor("psV0", [128, 512], F32) as _v0, \
                nc.psum_tensor("psV1", [128, 512], F32) as _v1:
            psV = [_v0.ap(), _v1.ap()]

            @block.gpsimd
            def _(g):
                p.wm(g, "Aend_sV")
                p.wm(g, "Aend_sP")
                p.wm(g, ("cv_wo", 15))
                p.wm(g, ("cv_wo", 14))
                srcv = wvTd.rearrange("(c p) e -> p c e", p=128)
                dstv = wvTs.rearrange("p (c e) -> p c e", c=16)
                p.inc(g.dma_start(out=dstv, in_=srcv), "wvl", 16)
                p.mark("wvld", "wvl")

            for m in range(NKT):
                @block.tensor
                def _(t, m=m):
                    if m == 0:
                        p.wm(t, "wvld")
                        p.wm(t, "Aend_sV")
                        p.wm(t, ("cv_wo", 15))
                        p.wm(t, ("cv_wo", 14))
                    if m >= 2:
                        p.wm(t, ("Vevac", m - 2))
                    inst = None
                    for dc in range(16):
                        inst = t.matmul(
                            out=psV[m % 2],
                            lhsT=xln1T[:, dc * 2048 + m * 128:
                                       dc * 2048 + (m + 1) * 128],
                            rhs=wvTs[:, dc * 512:(dc + 1) * 512],
                            start=(dc == 0), stop=(dc == 15))
                    p.inc(inst, "sP")
                    p.mark(("Vmm", m), "sP")

                @block.scalar
                def _(s, m=m):
                    p.w(s, "sA")
                    if m == 0:
                        p.wm(s, ("cv_wo", 15))
                        p.wm(s, ("cv_wo", 14))
                    p.wm(s, ("Vmm", m))
                    p.inc(s.activation(out=V[:, m * 512:(m + 1) * 512],
                                       in_=psV[m % 2], func=AF.Copy), "sA")
                    p.mark(("Vevac", m), "sA")

                @block.vector
                def _(v):
                    scale_dve(v, 2)

                @block.sync
                def _(sync):
                    scale_loads(sync, 2)

            p.mark("B1end_sP", "sP")
            p.mark("B1end_sA", "sA")

        if upto < 3:
            raise _Stop

        @block.sync
        def _(sync):
            p.wm(sync, "B1end_sP")
            p.wm(sync, "B1end_sA")
            p.wm(sync, "Aend_sV")
            for dst, srcd in [(cosk, coskd), (sink, sinkd)]:
                p.inc(sync.dma_start(out=dst, in_=srcd), "rtk", 16)
            p.mark("ropetab", "rtk")

        def hb1(idx, width, wTd, tag, pp, ppo, so, lag):
            nw = width // 512

            @block.gpsimd
            def _(g, idx=idx):
                if ("proj" + tag, idx - 2) in p.marks:
                    p.wm(g, ("proj" + tag, idx - 2))
                if tag == "Q" and idx < 2:
                    p.wm(g, ("projK", HK - 2 + idx))
                srcv = wTd[:, idx * 128:(idx + 1) * 128].rearrange(
                    "(c p) e -> p c e", p=128)
                dstw = wst[idx % 2].rearrange("p (c e) -> p c e", c=16)
                p.inc(g.dma_start(out=dstw, in_=srcv), f"wstl{idx % 2}", 16)
                p.mark(("wld" + tag, idx), f"wstl{idx % 2}")

            @block.tensor
            def _(t_, idx=idx):
                p.wm(t_, ("wld" + tag, idx))
                if idx == 0 and tag == "K":
                    p.wm(t_, "B1end_sA")
                    p.wm(t_, "ropetab")
                    p.wm(t_, "biasB")
                if idx == 0 and tag == "Q":
                    p.wm(t_, ("sqrtK", HK - 1))
                    p.wm(t_, ("appK", HK - 1))
                    p.wm(t_, ("nbK", HK - 1))
                    p.wm(t_, ("finK", HK - 1))
                if ("sqrt" + tag, idx - 2) in p.marks:
                    p.wm(t_, ("sqrt" + tag, idx - 2))
                if lag == 1 and ("t2" + tag, idx - 1) in p.marks:
                    p.wm(t_, ("t2" + tag, idx - 1))
                inst = None
                xoff = 0 if width == 2048 else 1024
                for dc in range(16):
                    for qq in range(nw):
                        inst = t_.matmul(
                            out=pp[:, qq * 512:(qq + 1) * 512],
                            lhsT=wst[idx % 2][:, dc * 128:(dc + 1) * 128],
                            rhs=xln1T[:, dc * 2048 + xoff + qq * 512:
                                      dc * 2048 + xoff + (qq + 1) * 512],
                            start=(dc == 0), stop=(dc == 15))
                p.inc(inst, "sP")
                p.mark(("proj" + tag, idx), "sP")

            @block.scalar
            def _(s, idx=idx):
                p.w(s, "sA")
                p.wm(s, ("proj" + tag, idx))
                for key in (("app" + tag, idx - lag), ("nb" + tag, idx - lag)):
                    if key in p.marks:
                        p.wm(s, key)
                p.inc(s.activation(out=sA[:, so:so + width],
                                   in_=pp[:, 0:width], func=AF.Copy), "sA")
                p.w(s, "sA")
                p.inc(s.activation(out=sBF[:, so:so + width],
                                   in_=sA[:, so:so + width], func=AF.Copy),
                      "sA")
                p.mark(("raw" + tag, idx), "sA")

            @block.tensor
            def _(t_, idx=idx):
                p.wm(t_, ("raw" + tag, idx))
                if lag == 1:
                    if ("sqrt" + tag, idx - 1) in p.marks:
                        p.wm(t_, ("sqrt" + tag, idx - 1))
                else:
                    if ("t2" + tag, idx - 2) in p.marks:
                        p.wm(t_, ("t2" + tag, idx - 2))
                inst = None
                for qq in range(nw):
                    inst = t_.matmul(
                        out=ppo[:, qq * 512:(qq + 1) * 512],
                        lhsT=rt, rhs=sBF[:, so + qq * 512:so + (qq + 1) * 512],
                        start=True, stop=True)
                p.inc(inst, "sP")
                p.mark(("rq" + tag, idx), "sP")

        def hb2a(idx, width, cos_t, sin_t, tag, pp, ppo, so):
            nw = width // 512

            if tag == "Q":
                @block.gpsimd
                def _(g, idx=idx):
                    p.w(g, "sPo")
                    p.wm(g, ("raw" + tag, idx))
                    if idx == 0:
                        p.wm(g, "ropetabq")
                    if ("fin" + tag, idx - 2) in p.marks:
                        p.wm(g, ("fin" + tag, idx - 2))
                    p.inc(g.tensor_tensor(out=sB[:, so:so + width],
                                          in0=sA[:, so:so + width],
                                          in1=cos_t, op=ALU.mult), "sPo")
                    p.mark(("t1" + tag, idx), "sPo")

                @block.vector
                def _(v, idx=idx):
                    p.w(v, "sV")
                    p.wm(v, ("t1" + tag, idx))
                    p.wm(v, ("rq" + tag, idx))
                    if idx == 0:
                        p.wm(v, "ropetabq")
                    p.inc(v.tensor_tensor(out=sA[:, so:so + width],
                                          in0=ppo[:, 0:width],
                                          in1=sin_t, op=ALU.mult), "sV")
                    p.mark(("t2" + tag, idx), "sV")
                    scale_dve(v, 2)

                @block.gpsimd
                def _(g, idx=idx):
                    p.w(g, "sPo")
                    p.wm(g, ("t2" + tag, idx))
                    if ("app" + tag, idx - 2) in p.marks:
                        p.wm(g, ("app" + tag, idx - 2))
                    p.inc(g.tensor_tensor(out=sC[:, so:so + width],
                                          in0=sB[:, so:so + width],
                                          in1=sA[:, so:so + width],
                                          op=ALU.add), "sPo")
                    p.mark(("ro" + tag, idx), "sPo")
            else:
                @block.vector
                def _(v, idx=idx):
                    p.w(v, "sV")
                    p.wm(v, ("raw" + tag, idx))
                    p.inc(v.tensor_tensor(out=sB[:, so:so + width],
                                          in0=sA[:, so:so + width],
                                          in1=cos_t, op=ALU.mult), "sV")
                    p.mark(("t1" + tag, idx), "sV")
                    p.wm(v, ("rq" + tag, idx))
                    p.w(v, "sV")
                    p.inc(v.tensor_tensor(out=sA[:, so:so + width],
                                          in0=ppo[:, 0:width],
                                          in1=sin_t, op=ALU.mult), "sV")
                    p.mark(("t2" + tag, idx), "sV")
                    p.w(v, "sV")
                    p.inc(v.tensor_tensor(out=sC[:, so:so + width],
                                          in0=sB[:, so:so + width],
                                          in1=sA[:, so:so + width],
                                          op=ALU.add), "sV")
                    p.mark(("ro" + tag, idx), "sV")
                    scale_dve(v, 2)

            @block.scalar
            def _(s, idx=idx):
                p.w(s, "sA")
                p.wm(s, ("ro" + tag, idx))
                p.inc(s.activation(out=sBF[:, so:so + width],
                                   in_=sC[:, so:so + width],
                                   func=AF.Square), "sA")
                p.mark(("sq" + tag, idx), "sA")

            @block.tensor
            def _(t_, idx=idx):
                p.wm(t_, ("sq" + tag, idx))
                p.wm(t_, ("t2" + tag, idx))
                inst = None
                for qq in range(nw):
                    inst = t_.matmul(
                        out=pp[:, qq * 512:(qq + 1) * 512],
                        lhsT=onesm,
                        rhs=sBF[:, so + qq * 512:so + (qq + 1) * 512],
                        start=True, stop=True)
                p.inc(inst, "sP")
                p.mark(("nb" + tag, idx), "sP")

            @block.scalar
            def _(s, idx=idx):
                p.w(s, "sA")
                p.wm(s, ("nb" + tag, idx))
                p.inc(s.activation(out=sA[:, so:so + width],
                                   in_=pp[:, 0:width], func=AF.Sqrt), "sA")
                p.mark(("sqrt" + tag, idx), "sA")

        def hb2b(idx, width, tag, so):
            @block.vector
            def _(v, idx=idx):
                p.w(v, "sV")
                p.wm(v, ("sqrt" + tag, idx))
                p.inc(v.tensor_scalar(out=sB[:, so:so + width],
                                      in0=sA[:, so:so + width],
                                      scalar1=EPS_QK, scalar2=None,
                                      op0=ALU.add), "sV")
                p.w(v, "sV")
                p.inc(v.reciprocal(out=sA[:, so:so + width],
                                   in_=sB[:, so:so + width]), "sV")
                p.w(v, "sV")
                p.inc(v.tensor_tensor(out=sB[:, so:so + width],
                                      in0=sC[:, so:so + width],
                                      in1=sA[:, so:so + width], op=ALU.mult),
                      "sV")
                p.mark(("app" + tag, idx), "sV")

        def hb2c(idx, width, gam_t, dst, dst_off, tag, so):
            @block.scalar
            def _(s, idx=idx):
                p.w(s, "sA")
                p.wm(s, ("app" + tag, idx))
                p.inc(s.activation(out=dst[:, dst_off:dst_off + width],
                                   in_=sB[:, so:so + width], func=AF.Copy,
                                   scale=gam_t), "sA")
                p.mark(("fin" + tag, idx), "sA")

        with nc.psum_tensor("psB0", [128, 2048], F32) as _b0, \
                nc.psum_tensor("psB1", [128, 2048], F32) as _b1:
            psb = [_b0.ap(), _b1.ap()]
            for kv in range(HK):
                hb1(kv, 2048, wkTd, "K", psb[kv % 2], psb[1 - kv % 2], 0, 1)
                hb2a(kv, 2048, cosk, sink, "K",
                     psb[kv % 2], psb[1 - kv % 2], 0)
                hb2b(kv, 2048, "K", 0)
                hb2c(kv, 2048, gamk, K, kv * 2048, "K", 0)

                @block.sync
                def _(sync):
                    scale_loads(sync, 2)

        if upto < 3.2:
            raise _Stop

        @block.sync
        def _(sync):
            p.wm(sync, ("roK", HK - 1))
            for dst, srcd in [(cosq, cosqd), (sinq, sinqd)]:
                p.inc(sync.dma_start(out=dst, in_=srcd), "rtq", 16)
            p.mark("ropetabq", "rtq")

        with nc.psum_tensor("psQ0", [128, 1024], F32) as _q0, \
                nc.psum_tensor("psQ1", [128, 1024], F32) as _q1, \
                nc.psum_tensor("psQ2", [128, 1024], F32) as _q2, \
                nc.psum_tensor("psQ3", [128, 1024], F32) as _q3:
            psq = [_q0.ap(), _q1.ap(), _q2.ap(), _q3.ap()]

            def qargs(h):
                pq = h % 2
                return (psq[pq * 2], psq[pq * 2 + 1], pq * 1024)

            hb1(0, 1024, wqTd, "Q", *qargs(0), 2)
            for h in range(1, H):
                if h >= 2:
                    hb2b(h - 2, 1024, "Q", (h % 2) * 1024)
                pp, ppo, so = qargs(h)
                hb1(h, 1024, wqTd, "Q", pp, ppo, so, 2)
                if h >= 2:
                    hb2c(h - 2, 1024, gamq, Q, (h - 2) * 1024, "Q",
                         (h % 2) * 1024)
                ppp, pppo, pso = qargs(h - 1)
                hb2a(h - 1, 1024, cosq, sinq, "Q", ppp, pppo, pso)

                @block.sync
                def _(sync):
                    scale_loads(sync, 2)
            hb2a(H - 1, 1024, cosq, sinq, "Q", *qargs(H - 1))
            for hh in (H - 2, H - 1):
                hb2b(hh, 1024, "Q", (hh % 2) * 1024)
                hb2c(hh, 1024, gamq, Q, hh * 1024, "Q", (hh % 2) * 1024)

        p.mark("B2end_sP", "sP")
        p.mark("B2end_sV", "sV")
        p.mark("B2end_sA", "sA")

        if upto < 4:
            raise _Stop
        with nc.psum_tensor("psST0", [128, 1152], F32) as _s0, \
                nc.psum_tensor("psST1", [128, 1152], F32) as _s1, \
                nc.psum_tensor("psO0", [128, 128], F32) as _o0, \
                nc.psum_tensor("psO1", [128, 128], F32) as _o1:
            psST = [_s0.ap(), _s1.ap()]
            psO = [_o0.ap(), _o1.ap()]

            @block.sync
            def _(sync):
                p.wm(sync, "B2end_sA")
                p.wm(sync, "B2end_sV")
                p.wm(sync, "B2end_sP")
                p.inc(sync.dma_start(out=masks, in_=m01d), "mskl", 16)
                p.mark("maskld", "mskl")

            NSTR = H * NQT

            def emit_st(t, n):
                h, i = n // 8, n % 8
                kv = h // 4
                j = n % 2
                if n == 0:
                    p.wm(t, "B2end_sA")
                if n >= 2:
                    p.wm(t, ("Cdencp", n - 2))
                    if (n - 2) % 8 == 7:
                        p.wm(t, ("Cnormrd", (n - 2) // 8))
                inst = None
                for s_ in range(NSLOT):
                    mt = i + s_
                    inst = t.matmul(
                        out=psST[j][:, s_ * 128:(s_ + 1) * 128],
                        lhsT=K[:, kv * 2048 + mt * 128:kv * 2048 + (mt + 1) * 128],
                        rhs=Q[:, h * 1024 + i * 128:h * 1024 + (i + 1) * 128],
                        start=True, stop=True)
                p.inc(inst, "sP")
                p.mark(("Cst", n), "sP")

            @block.tensor
            def _(t):
                emit_st(t, 0)

            for n in range(NSTR):
                h, i = n // 8, n % 8
                kv = h // 4
                j = n % 2

                @block.tensor
                def _(t, n=n):
                    if n + 1 < NSTR:
                        emit_st(t, n + 1)

                @block.scalar
                def _(s, j=j, n=n):
                    p.w(s, "sA")
                    p.wm(s, ("Cst", n))
                    if n == 0:
                        p.wm(s, "maskld")
                    if n >= 2:
                        p.wm(s, ("Cptm", n - 2))
                    p.inc(s.activation(out=ptraw[j], in_=psST[j],
                                       func=AF.Exp, scale=SCALE,
                                       bias=biasB), "sA")
                    p.mark(("Cexp", n), "sA")

                @block.sync
                def _(sync):
                    scale_loads(sync, 1)

                @block.vector
                def _(v, i=i, j=j, n=n):
                    p.w(v, "sV")
                    p.wm(v, ("Cexp", n))
                    if n >= 2:
                        p.wm(v, ("Cpv", n - 2))
                    p.inc(v.tensor_tensor(
                        out=ptm[j], in0=ptraw[j],
                        in1=masks[:, i * 1152:(i + 1) * 1152],
                        op=ALU.mult), "sV")
                    p.mark(("Cptm", n), "sV")
                    scale_dve(v, 1)

                @block.tensor
                def _(t, j=j, n=n, kv=kv, i=i):
                    p.wm(t, ("Cptm", n))
                    inst = None
                    for s_ in range(NSLOT):
                        inst = t.matmul(
                            out=psST[j][0:1, 0:128], lhsT=onesc,
                            rhs=ptm[j][:, s_ * 128:(s_ + 1) * 128],
                            start=(s_ == 0), stop=(s_ == NSLOT - 1))
                    p.inc(inst, "sP")
                    p.mark(("Cden", n), "sP")
                    if n >= 2:
                        p.wm(t, ("Coe", n - 2))
                    for s_ in range(NSLOT):
                        mt = i + s_
                        inst = t.matmul(
                            out=psO[j],
                            lhsT=V[:, mt * 512 + kv * 128:
                                   mt * 512 + (kv + 1) * 128],
                            rhs=ptm[j][:, s_ * 128:(s_ + 1) * 128],
                            start=(s_ == 0), stop=(s_ == NSLOT - 1))
                    p.inc(inst, "sP")
                    p.mark(("Cpv", n), "sP")

                @block.vector
                def _(v, j=j, n=n, i=i):
                    p.w(v, "sV")
                    p.wm(v, ("Cden", n))
                    p.inc(v.tensor_copy(out=den_sb[0:1, i * 128:(i + 1) * 128],
                                        in_=psST[j][0:1, 0:128]), "sV")
                    p.mark(("Cdencp", n), "sV")

                @block.vector
                def _(v, j=j, n=n, h=h, i=i):
                    p.w(v, "sV")
                    p.wm(v, ("Cpv", n))
                    p.inc(v.tensor_copy(
                        out=attnout[:, h * 1024 + i * 128:
                                    h * 1024 + (i + 1) * 128],
                        in_=psO[j]), "sV")
                    p.mark(("Coe", n), "sV")

                if i == 7:
                    jb = n % 2

                    @block.vector
                    def _(v, n=n):
                        p.w(v, "sV")
                        if n // 8 >= 1:
                            p.wm(v, ("Cbc", n // 8 - 1))
                        p.inc(v.reciprocal(out=rden_sb[0:1, 0:1024],
                                           in_=den_sb[0:1, 0:1024]), "sV")
                        p.mark(("Crecip", n // 8), "sV")

                    @block.tensor
                    def _(t, n=n, jb=jb):
                        p.wm(t, ("Crecip", n // 8))
                        inst = None
                        for qq in range(2):
                            inst = t.matmul(
                                out=psST[jb][:, qq * 512:(qq + 1) * 512],
                                lhsT=onesrf[0:1, :],
                                rhs=rden_sb[0:1, qq * 512:(qq + 1) * 512],
                                start=True, stop=True)
                        p.inc(inst, "sP")
                        p.mark(("Cbc", n // 8), "sP")

                    @block.vector
                    def _(v, n=n, h=h, jb=jb):
                        p.w(v, "sV")
                        p.wm(v, ("Cbc", n // 8))
                        for k2 in range(8):
                            p.wm(v, ("Coe", n - 7 + k2))
                        sl = attnout[:, h * 1024:(h + 1) * 1024]
                        p.inc(v.tensor_tensor(out=sl, in0=sl,
                                              in1=psST[jb][:, 0:1024],
                                              op=ALU.mult), "sV")
                        p.mark(("Cnormrd", h), "sV")

            p.mark("Cend_sP", "sP")
            p.mark("Cend_sV", "sV")
            p.mark("Cend_sA", "sA")

        if upto < 5:
            raise _Stop
        with nc.psum_tensor("psY0", [128, 2048], F32) as _y0, \
                nc.psum_tensor("psY1", [128, 2048], F32) as _y1:
            psY = [_y0.ap(), _y1.ap()]

            def emit_tr(t, g):
                for jj in (2 * g, 2 * g + 1):
                    jv = jj % 2
                    p.wm(t, ("Dl2", jj))
                    if g < 3:
                        p.wm(t, ("Dycopy", 2 * g + 2))
                        p.wm(t, ("Dycopy", 2 * g + 3))
                    inst = None
                    for c in range(16):
                        inst = t.matmul(
                            out=psY[jv][:, c * 128:(c + 1) * 128],
                            lhsT=xl2tm[jv][:, c * 128:(c + 1) * 128],
                            rhs=ident, start=True, stop=True)
                    p.inc(inst, "sP")
                    p.mark(("Dl2tr", jj), "sP")

            def emit_trevac(v, g):
                for jj in (2 * g, 2 * g + 1):
                    jv = jj % 2
                    p.w(v, "sV")
                    p.wm(v, ("Dl2tr", jj))
                    dst = xln2T.rearrange("p (c n) -> p c n", c=16)[
                        :, :, jj * 128:(jj + 1) * 128]
                    srcv = psY[jv].rearrange("p (c n) -> p c n", c=16)
                    p.inc(v.tensor_copy(out=dst, in_=srcv), "sV")
                    p.mark(("Dtrevac", jj), "sV")

            for grp in range(4):
                j0, j1 = 2 * grp, 2 * grp + 1

                @block.sync
                def _(sync, grp=grp, j0=j0, j1=j1):
                    if grp == 0:
                        p.wm(sync, "Cend_sV")
                        p.wm(sync, "Cend_sA")
                        p.wm(sync, "Cend_sP")
                    for jj, buf in ((j0, xr[0]), (j1, xr[1])):
                        if grp >= 1:
                            p.wm(sync, ("Dx2st", jj - 2))
                        p.inc(sync.dma_start(
                            out=buf, in_=xd[1024 + jj * 128:
                                            1024 + (jj + 1) * 128, :]),
                            f"xrl{jj % 2}", 16)
                        p.mark(("Dxld", jj), f"xrl{jj % 2}")
                    scale_loads(sync, 4)

                for ec in range(16):
                    gec = grp * 16 + ec

                    @block.sync
                    def _(sync, grp=grp, ec=ec, gec=gec):
                        if gec == 0:
                            p.wm(sync, "Cend_sV")
                            p.wm(sync, "Cend_sP")
                            p.wm(sync, "Cend_sA")
                            p.wm(sync, ("cv_wo", 15))
                        if gec >= 2:
                            p.wm(sync, ("Dwous", gec - 2))
                        srcw = wobf[ec * 128:(ec + 1) * 128, :]
                        p.inc(sync.dma_start(out=wotb[ec % 2], in_=srcw),
                              f"wotl{ec % 2}", 16)
                        p.mark(("Dwcast", gec), f"wotl{ec % 2}")
                        p.mark(("Dwold", gec), f"wotl{ec % 2}")
                        scale_loads(sync, 1)

                    @block.vector
                    def _(v):
                        scale_dve(v, 1)

                    @block.tensor
                    def _(t, grp=grp, ec=ec, j0=j0, j1=j1, gec=gec):
                        p.wm(t, ("Dwcast", gec))
                        if ec == 0:
                            if grp >= 1:
                                p.wm(t, ("Dycopy", j0 - 2))
                                p.wm(t, ("Dycopy", j1 - 2))
                            if grp >= 2:
                                p.wm(t, ("Dtrevac", j0 - 4))
                                p.wm(t, ("Dtrevac", j1 - 4))
                        inst = None
                        for jj, pps in ((j0, psY[0]), (j1, psY[1])):
                            lo = ec * 1024 + jj * 128
                            for qq in range(4):
                                inst = t.matmul(
                                    out=pps[:, qq * 512:(qq + 1) * 512],
                                    lhsT=attnout[:, lo:lo + 128],
                                    rhs=wotb[ec % 2][:, qq * 512:(qq + 1) * 512],
                                    start=(ec == 0), stop=(ec == 15))
                        p.inc(inst, "sP")
                        p.mark(("Dwous", gec), "sP")
                        if ec == 15:
                            p.mark(("Dyp", j0), "sP")
                            p.mark(("Dyp", j1), "sP")

                for jj in (j0, j1):
                    jv = jj % 2

                    @block.vector
                    def _(v, jj=jj, jv=jv):
                        p.w(v, "sV")
                        p.wm(v, ("Dyp", jj))
                        if jj >= 2:
                            p.wm(v, ("Dl2", jj - 2))
                        p.inc(v.tensor_copy(out=ybuf[jv], in_=psY[jv]), "sV")
                        p.mark(("Dycopy", jj), "sV")

                if grp >= 1:
                    @block.tensor
                    def _(t, grp=grp):
                        emit_tr(t, grp - 1)

                    @block.vector
                    def _(v, grp=grp):
                        emit_trevac(v, grp - 1)

                for jj in (j0, j1):
                    jv = jj % 2

                    @block.vector
                    def _(v, jj=jj, jv=jv):
                        p.w(v, "sV")
                        p.wm(v, ("Dycopy", jj))
                        ln_stats(v, ybuf[jv], mvv[jv])
                        p.mark(("Dmv", jj), "sV")

                    @block.scalar
                    def _(s, jj=jj, jv=jv):
                        p.w(s, "sA")
                        p.wm(s, ("Dmv", jj))
                        p.inc(s.activation(out=stdv[:, jv:jv + 1],
                                           in_=mvv[jv][:, 1:2], func=AF.Sqrt,
                                           bias=epsln), "sA")
                        p.mark(("Dstd", jj), "sA")

                    @block.vector
                    def _(v, jj=jj, jv=jv):
                        p.w(v, "sV")
                        p.wm(v, ("Dstd", jj))
                        p.inc(v.reciprocal(out=rstdv[:, jv:jv + 1],
                                           in_=stdv[:, jv:jv + 1]), "sV")
                        p.w(v, "sV")
                        p.inc(v.tensor_scalar(out=ytile[jv], in0=ybuf[jv],
                                              scalar1=mvv[jv][:, 0:1],
                                              scalar2=rstdv[:, jv:jv + 1],
                                              op0=ALU.subtract, op1=ALU.mult),
                              "sV")
                        p.mark(("Dyt", jj), "sV")
                        p.wm(v, ("Dxld", jj))
                        if jj >= 2:
                            p.wm(v, ("Dx2st", jj - 2))
                        p.w(v, "sV")
                        p.inc(v.tensor_tensor(out=x2t[jv], in0=ytile[jv],
                                              in1=xr[jv], op=ALU.add), "sV")
                        p.mark(("Dx2", jj), "sV")
                        ln_stats(v, x2t[jv], mvv[jv])
                        p.mark(("Dmv2", jj), "sV")

                    @block.sync
                    def _(sync, jj=jj, jv=jv):
                        p.wm(sync, ("Dx2", jj))
                        p.inc(sync.dma_start(
                            out=x2scr[jj * 128:(jj + 1) * 128, :],
                            in_=x2t[jv]), f"x2s{jv}", 16)
                        p.mark(("Dx2st", jj), f"x2s{jv}")

                    @block.scalar
                    def _(s, jj=jj, jv=jv):
                        p.w(s, "sA")
                        p.wm(s, ("Dmv2", jj))
                        p.inc(s.activation(out=stdv[:, jv:jv + 1],
                                           in_=mvv[jv][:, 1:2], func=AF.Sqrt,
                                           bias=epsln), "sA")
                        p.mark(("Dstd2", jj), "sA")

                    @block.vector
                    def _(v, jj=jj, jv=jv):
                        p.w(v, "sV")
                        p.wm(v, ("Dstd2", jj))
                        p.inc(v.reciprocal(out=rstdv[:, jv:jv + 1],
                                           in_=stdv[:, jv:jv + 1]), "sV")
                        p.w(v, "sV")
                        if jj >= 2:
                            p.wm(v, ("Dl2tr", jj - 2))
                        p.inc(v.tensor_scalar(out=xl2tm[jv], in0=x2t[jv],
                                              scalar1=mvv[jv][:, 0:1],
                                              scalar2=rstdv[:, jv:jv + 1],
                                              op0=ALU.subtract, op1=ALU.mult),
                              "sV")
                        p.mark(("Dl2", jj), "sV")
                        scale_dve(v, 1)

            @block.tensor
            def _(t):
                emit_tr(t, 3)

            @block.vector
            def _(v):
                emit_trevac(v, 3)

            while sc["ld"] < NSC2 or sc["act"] < NSC2:
                @block.sync
                def _(sync):
                    scale_loads(sync, 4)

                @block.vector
                def _(v):
                    scale_dve(v, 4)

            @block.vector
            def _(v):
                p.w(v, "sV")
                p.w(v, "sA")
                p.inc(v.reduce_sum(out=tiny[:, 3:4], in_=sstr1, axis=AX.X),
                      "sV")
                p.mark(("th1red", 0), "sV")

            @block.tensor
            def _(t):
                p.wm(t, ("th1red", 0))
                p.wm(t, ("Dtrevac", 6))
                p.inc(t.matmul(out=psY[0][0:1, 4:5], lhsT=onescf,
                               rhs=tiny[:, 3:4], start=True, stop=True), "sP")
                p.mark(("th1mm", 0), "sP")

            @block.vector
            def _(v):
                p.w(v, "sV")
                p.wm(v, ("th1mm", 0))
                p.inc(v.tensor_copy(out=trow[0:1, 3:4],
                                    in_=psY[0][0:1, 4:5]), "sV")
                p.mark(("th1row", 0), "sV")

            @block.tensor
            def _(t):
                p.wm(t, ("th1row", 0))
                p.inc(t.matmul(out=psY[0][:, 256:257], lhsT=onesrf[0:1, :],
                               rhs=trow[0:1, 3:4], start=True, stop=True),
                      "sP")
                p.mark(("th1bc", 0), "sP")

            @block.vector
            def _(v):
                p.w(v, "sV")
                p.wm(v, ("th1bc", 0))
                _c = 0.7 / float(DFF * DM)
                _hi = float(np.float32(_c))
                _lo = float(np.float32(_c - _hi))
                p.inc(v.tensor_scalar(out=theta1, in0=psY[0][:, 256:257],
                                      scalar1=_hi, scalar2=None,
                                      op0=ALU.mult), "sV")
                p.w(v, "sV")
                p.inc(v.scalar_tensor_tensor(out=theta1,
                                             in0=psY[0][:, 256:257],
                                             scalar=_lo, in1=theta1,
                                             op0=ALU.mult, op1=ALU.add),
                      "sV")
                p.w(v, "sV")
                p.inc(v.tensor_scalar(out=negth1, in0=theta1, scalar1=-1.0,
                                      scalar2=None, op0=ALU.mult), "sV")
                p.mark(("th1done", 0), "sV")

            @block.vector
            def _(v):
                p.w(v, "sV")
                p.inc(v.reduce_sum(out=tiny[:, 4:5], in_=sstr2, axis=AX.X),
                      "sV")
                p.mark(("th2red", 0), "sV")

            @block.tensor
            def _(t):
                p.wm(t, ("th2red", 0))
                p.inc(t.matmul(out=psY[0][0:1, 0:1], lhsT=onescf,
                               rhs=tiny[:, 4:5], start=True, stop=True), "sP")
                p.mark(("th2mm", 0), "sP")

            @block.vector
            def _(v):
                p.w(v, "sV")
                p.wm(v, ("th2mm", 0))
                p.inc(v.tensor_copy(out=trow[0:1, 4:5],
                                    in_=psY[0][0:1, 0:1]), "sV")
                p.mark(("th2row", 0), "sV")

            @block.tensor
            def _(t):
                p.wm(t, ("th2row", 0))
                p.inc(t.matmul(out=psY[0][:, 512:513], lhsT=onesrf[0:1, :],
                               rhs=trow[0:1, 4:5], start=True, stop=True),
                      "sP")
                p.mark(("th2bc", 0), "sP")

            @block.vector
            def _(v):
                p.w(v, "sV")
                p.wm(v, ("th2bc", 0))
                _c = 0.7 / float(DFF * DM)
                _hi = float(np.float32(_c))
                _lo = float(np.float32(_c - _hi))
                p.inc(v.tensor_scalar(out=theta2, in0=psY[0][:, 512:513],
                                      scalar1=_hi, scalar2=None,
                                      op0=ALU.mult), "sV")
                p.w(v, "sV")
                p.inc(v.scalar_tensor_tensor(out=theta2,
                                             in0=psY[0][:, 512:513],
                                             scalar=_lo, in1=theta2,
                                             op0=ALU.mult, op1=ALU.add),
                      "sV")
                p.w(v, "sV")
                p.inc(v.tensor_scalar(out=negth2, in0=theta2, scalar1=-1.0,
                                      scalar2=None, op0=ALU.mult), "sV")
                p.mark(("th2done", 0), "sV")

            p.mark("Dend_sP", "sP")
            p.mark("Dend_sV", "sV")
            p.mark("Dend_sA", "sA")
            p.mark("Dend_sPo", "sPo")

        if upto < 6:
            raise _Stop
        with nc.psum_tensor("psH", [128, 1024], F32) as _h0, \
                nc.psum_tensor("psOU0", [128, 1024], F32) as _u0, \
                nc.psum_tensor("psOU1", [128, 1024], F32) as _u1:
            psHq = [_h0.ap()[:, 0:512], _h0.ap()[:, 512:1024]]
            psOU = [_u0.ap(), _u1.ap()]

            @block.vector
            def _(v):
                p.w(v, "sV")
                p.wm(v, "Dend_sV")
                inst = None
                for dc in range(16):
                    inst = v.tensor_scalar(
                        out=out_acc[:, dc * 1024:(dc + 1) * 1024],
                        in0=out_acc[:, dc * 1024:(dc + 1) * 1024],
                        scalar1=0.0, scalar2=b2sb[:, dc:dc + 1],
                        op0=ALU.mult, op1=ALU.add)
                p.inc(inst, "sV")
                p.mark("Einit", "sV")

            for fc in range(8):
                for k in range(8):
                    fs = fc * 8 + k
                    jj = fs % 2

                    @block.sync
                    def _(sync, fs=fs, jj=jj):
                        if fs == 0:
                            p.wm(sync, "Dend_sP")
                            p.wm(sync, "Dend_sV")
                            p.wm(sync, "Dend_sA")
                            p.wm(sync, "Dend_sPo")
                            p.w(sync, "x2s0")
                            p.w(sync, "x2s1")
                        if fs >= 2:
                            p.wm(sync, ("Estt1", fs - 2))
                        p.inc(sync.dma_start(
                            out=w1b[jj], in_=w1pd[fs * 128:(fs + 1) * 128, :]),
                            f"w1l{jj}", 16)
                        p.mark(("Ew1ld", fs), f"w1l{jj}")
                        if fs >= 2:
                            p.wm(sync, ("Estt2", fs - 2))
                        p.inc(sync.dma_start(
                            out=w2b[jj], in_=w2Td[fs * 128:(fs + 1) * 128, :]),
                            f"w2l{jj}", 16)
                        p.mark(("Ew2ld", fs), f"w2l{jj}")

                    @block.vector
                    def _(v, fs=fs, jj=jj):
                        p.w(v, "sV")
                        p.wm(v, ("Ew1ld", fs))
                        if fs == 0:
                            p.wm(v, ("th1done", 0))
                        if fs >= 2:
                            p.wm(v, ("Ehmm", fs - 2, 1))
                        p.inc(v.tensor_scalar(out=a1m, in0=w1b[jj],
                                              scalar1=theta1, scalar2=None,
                                              op0=ALU.is_ge), "sV")
                        p.w(v, "sV")
                        p.inc(v.scalar_tensor_tensor(
                            out=w1qn[jj], in0=w1b[jj], scalar=negth1,
                            in1=a1m, op0=ALU.is_le, op1=ALU.subtract), "sV")
                        p.mark(("Estt1", fs), "sV")

                    @block.vector
                    def _(v, fs=fs, jj=jj, fc=fc):
                        p.w(v, "sV")
                        p.wm(v, ("Ew2ld", fs))
                        if fs == 0:
                            p.wm(v, ("th2done", 0))
                        if fc >= 1 and fs % 8 == 0:
                            p.wm(v, ("Eoumm", (fc - 1) * 16 + 15))
                        p.inc(v.tensor_scalar(out=a2m, in0=w2b[jj],
                                              scalar1=theta2, scalar2=None,
                                              op0=ALU.is_ge), "sV")
                        p.w(v, "sV")
                        p.inc(v.scalar_tensor_tensor(
                            out=w2qn[fs % 8], in0=w2b[jj], scalar=negth2,
                            in1=a2m, op0=ALU.is_le, op1=ALU.subtract), "sV")
                        p.mark(("Estt2", fs), "sV")

                    @block.tensor
                    def _(t, fs=fs, jj=jj):
                        p.wm(t, ("Estt1", fs))
                        for qq in range(2):
                            if fs >= 1:
                                p.wm(t, ("Egelu", fs - 1, qq))
                            inst = None
                            for dc in range(16):
                                inst = t.matmul(
                                    out=psHq[qq],
                                    lhsT=w1qn[jj][:, dc * 128:(dc + 1) * 128],
                                    rhs=xln2T[:, dc * 1024 + qq * 512:
                                              dc * 1024 + (qq + 1) * 512],
                                    start=(dc == 0), stop=(dc == 15))
                            p.inc(inst, "sP")
                            p.mark(("Ehmm", fs, qq), "sP")

                    @block.scalar
                    def _(s, fs=fs, k=k, fc=fc):
                        for qq in range(2):
                            p.w(s, "sA")
                            p.wm(s, ("Ehmm", fs, qq))
                            if fc >= 1 and k == 0 and qq == 0:
                                p.wm(s, ("Eoumm", (fc - 1) * 16 + 15))
                            p.inc(s.activation(
                                out=h_buf[:, k * 1024 + qq * 512:
                                          k * 1024 + (qq + 1) * 512],
                                in_=psHq[qq], func=AF.Gelu, scale=-1.0,
                                bias=b1sb[:, fs:fs + 1]), "sA")
                            p.mark(("Egelu", fs, qq), "sA")

                for dc in range(16):
                    ju = dc % 2

                    @block.tensor
                    def _(t, fc=fc, dc=dc, ju=ju):
                        if dc == 0:
                            p.wm(t, ("Egelu", fc * 8 + 7, 1))
                            for k2 in range(8):
                                p.wm(t, ("Estt2", fc * 8 + k2))
                        if fc * 16 + dc >= 2:
                            p.wm(t, ("Eacc", fc * 16 + dc - 2))
                        inst = None
                        for qq in range(2):
                            for k2 in range(8):
                                inst = t.matmul(
                                    out=psOU[ju][:, qq * 512:(qq + 1) * 512],
                                    lhsT=w2qn[k2][:, dc * 128:(dc + 1) * 128],
                                    rhs=h_buf[:, k2 * 1024 + qq * 512:
                                              k2 * 1024 + (qq + 1) * 512],
                                    start=(k2 == 0), stop=(k2 == 7))
                        p.inc(inst, "sP")
                        p.mark(("Eoumm", fc * 16 + dc), "sP")

                    @block.vector
                    def _(v, fc=fc, dc=dc, ju=ju):
                        p.w(v, "sV")
                        p.wm(v, ("Eoumm", fc * 16 + dc))
                        if fc == 0:
                            p.wm(v, "Einit")
                        acc = out_acc[:, dc * 1024:(dc + 1) * 1024]
                        p.inc(v.tensor_tensor(out=acc, in0=acc, in1=psOU[ju],
                                              op=ALU.subtract), "sV")
                        p.mark(("Eacc", fc * 16 + dc), "sV")

            p.mark("Eend_sP", "sP")
            p.mark("Eend_sV", "sV")
            p.mark("Eend_sA", "sA")

        if upto < 7:
            raise _Stop
        with nc.psum_tensor("psF0", [128, 2048], F32) as _f0, \
                nc.psum_tensor("psF1", [128, 2048], F32) as _f1:
            psF = [_f0.ap(), _f1.ap()]
            for jj in range(NQT):
                jv = jj % 2

                @block.sync
                def _(sync, jj=jj, jv=jv):
                    if jj == 0:
                        p.wm(sync, "Eend_sV")
                        p.w(sync, "x2s0")
                        p.w(sync, "x2s1")
                    if jj >= 2:
                        p.wm(sync, ("Fout", jj - 2))
                    p.inc(sync.dma_start(
                        out=x2b[jv], in_=x2scr[jj * 128:(jj + 1) * 128, :]),
                        f"x2l{jv}", 16)
                    p.mark(("Fx2ld", jj), f"x2l{jv}")

                @block.scalar
                def _(s, jj=jj, jv=jv):
                    p.w(s, "sA")
                    if jj == 0:
                        p.wm(s, "Eend_sV")
                    if jj >= 2:
                        p.wm(s, ("Ftr", jj - 2))
                    src = out_acc.rearrange("p (c n) -> p c n", c=16)[
                        :, :, jj * 128:(jj + 1) * 128]
                    dst = acb[jv].rearrange("p (c n) -> p c n", c=16)
                    p.inc(s.activation(out=dst, in_=src, func=AF.Copy), "sA")
                    p.mark(("Fcast", jj), "sA")

                @block.tensor
                def _(t, jj=jj, jv=jv):
                    p.wm(t, ("Fcast", jj))
                    if jj == 0:
                        p.wm(t, "Eend_sV")
                    if jj >= 2:
                        p.wm(t, ("Fapplied", jj - 2))
                    inst = None
                    for c in range(16):
                        inst = t.matmul(
                            out=psF[jv][:, c * 128:(c + 1) * 128],
                            lhsT=acb[jv][:, c * 128:(c + 1) * 128],
                            rhs=ident, start=True, stop=True)
                    p.inc(inst, "sP")
                    p.mark(("Ftr", jj), "sP")

                @block.vector
                def _(v, jj=jj, jv=jv):
                    p.w(v, "sV")
                    p.wm(v, ("Ftr", jj))
                    ln_stats(v, psF[jv], mvv[jv])
                    p.mark(("Fmv", jj), "sV")

                @block.scalar
                def _(s, jj=jj, jv=jv):
                    p.w(s, "sA")
                    p.wm(s, ("Fmv", jj))
                    p.inc(s.activation(out=stdv[:, jv:jv + 1],
                                       in_=mvv[jv][:, 1:2], func=AF.Sqrt,
                                       bias=epsln), "sA")
                    p.mark(("Fstd", jj), "sA")

                @block.vector
                def _(v, jj=jj, jv=jv):
                    p.w(v, "sV")
                    p.wm(v, ("Fstd", jj))
                    p.inc(v.reciprocal(out=rstdv[:, jv:jv + 1],
                                       in_=stdv[:, jv:jv + 1]), "sV")
                    p.w(v, "sV")
                    if jj >= 2:
                        p.wm(v, ("Fstore", jj - 2))
                    p.inc(v.tensor_scalar(out=outb[jv], in0=psF[jv],
                                          scalar1=mvv[jv][:, 0:1],
                                          scalar2=rstdv[:, jv:jv + 1],
                                          op0=ALU.subtract, op1=ALU.mult),
                          "sV")
                    p.mark(("Fapplied", jj), "sV")
                    p.wm(v, ("Fx2ld", jj))
                    p.w(v, "sV")
                    p.inc(v.tensor_tensor(out=outb[jv], in0=outb[jv],
                                          in1=x2b[jv], op=ALU.add), "sV")
                    p.mark(("Fout", jj), "sV")

                @block.sync
                def _(sync, jj=jj, jv=jv):
                    p.wm(sync, ("Fout", jj))
                    p.inc(sync.dma_start(
                        out=outd[jj * 128:(jj + 1) * 128, :], in_=outb[jv]),
                        f"outs{jv}", 16)
                    p.mark(("Fstore", jj), f"outs{jv}")

            @block.sync
            def _(sync):
                p.w(sync, "outs0")
                p.w(sync, "outs1")
    except _Stop:
        pass

    return nc


def _host_constants(half):
    f64 = np.float64
    inv = 1.0 / (10000.0 ** (np.arange(0, 128, 2, dtype=f64) / 128.0))

    def tables(pos):
        ang = pos[None, :] * inv[:, None]
        c = np.concatenate([np.cos(ang), np.cos(ang)], 0)
        s = np.concatenate([np.sin(ang), np.sin(ang)], 0)
        return (np.ascontiguousarray(c.astype(np.float32)),
                np.ascontiguousarray(s.astype(np.float32)))

    qpos = (half * 1024 + np.arange(TQ)).astype(f64)
    kpos = (((1 - half) * 1024 + np.arange(TKV)) % 2048).astype(f64)
    cosq, sinq = tables(qpos)
    cosk, sink = tables(kpos)

    m01 = np.zeros((128, NQT * NSLOT * 128), dtype=np.float32)
    kk = np.arange(128)
    for i in range(NQT):
        gq = half * 1024 + i * 128 + np.arange(128)
        for s in range(NSLOT):
            m = i + s
            gk = kpos[m * 128 + kk]
            diff = gq[None, :] - gk[:, None]
            m01[:, (i * NSLOT + s) * 128:(i * NSLOT + s + 1) * 128] = (
                (diff >= 0) & (diff <= 1024)).astype(np.float32)
    return cosq, sinq, cosk, sink, np.ascontiguousarray(
        m01.astype(ml_dtypes.bfloat16))


def make_in_maps(L0_attn_x, L0_attn_wq, L0_attn_wk, L0_attn_wv, L0_attn_wo,
                 L0_attn_gamma_q, L0_attn_gamma_k, L0_ffn_w1, L0_ffn_b1,
                 L0_ffn_w2, L0_ffn_b2):
    bf = ml_dtypes.bfloat16
    cc = np.ascontiguousarray
    shared = {
        "wqT": cc(np.asarray(L0_attn_wq, np.float32).T),
        "wkT": cc(np.asarray(L0_attn_wk, np.float32).T),
        "wvT": cc(np.asarray(L0_attn_wv, np.float32).T),
        "woT": cc(np.asarray(L0_attn_wo, np.float32).T),
        "w1": cc(np.asarray(L0_ffn_w1, np.float32)),
        "w1p": cc(np.asarray(L0_ffn_w1, np.float32)
                  .reshape(64, 128, 16, 128).transpose(0, 3, 2, 1)
                  .reshape(8192, 2048)),
        "w2T": cc(np.asarray(L0_ffn_w2, np.float32).T),
        "b1s": cc(np.asarray(L0_ffn_b1, np.float32).reshape(64, 128).T),
        "b2s": cc(np.asarray(L0_ffn_b2, np.float32).reshape(16, 128).T),
        "gamq": cc(np.asarray(L0_attn_gamma_q, np.float32).reshape(128, 1)),
        "gamk": cc(np.asarray(L0_attn_gamma_k, np.float32).reshape(128, 1)),
        "gamqr": cc(np.asarray(L0_attn_gamma_q, np.float32).reshape(1, 128)),
        "gamkr": cc(np.asarray(L0_attn_gamma_k, np.float32).reshape(1, 128)),
        "ident": np.eye(128, dtype=bf),
        "onesm": np.ones((128, 128), dtype=bf),
        "onesc": np.ones((128, 1), dtype=bf),
        "onescf": np.ones((128, 1), dtype=np.float32),
        "onesrf": np.ones((1, 128), dtype=np.float32),
    }
    rtm = np.zeros((128, 128), dtype=np.float32)
    for kidx in range(64):
        rtm[kidx, kidx + 64] = 1.0
    for kidx in range(64, 128):
        rtm[kidx, kidx - 64] = -1.0
    shared["rt"] = rtm.astype(bf)

    halfconst = {}
    for half in range(2):
        cosq, sinq, cosk, sink, m01 = _host_constants(half)
        halfconst[half] = {"cosq": cosq, "sinq": sinq, "cosk": cosk,
                           "sink": sink, "m01": m01}

    x = np.asarray(L0_attn_x, np.float32)
    in_maps = []
    for core in range(8):
        b, half = core // 2, core % 2
        d = dict(shared)
        d.update(halfconst[half])
        d["x"] = cc(np.roll(x[b], -(1 - half) * 1024, axis=0))
        in_maps.append(d)
    return in_maps


def kernel(**ins):
    in_maps = make_in_maps(**ins)
    nc = build_program()
    res = run_bass_kernel_spmd(nc, in_maps, core_ids=list(range(8)))
    out = np.empty((4, 2048, 2048), dtype=np.float32)
    for core in range(8):
        b, half = core // 2, core % 2
        out[b, half * 1024:(half + 1) * 1024, :] = res.results[core]["out"]
    return out


if __name__ == "__main__":
    import reference
    ins = reference.setup_inputs()
    ins = {k: np.asarray(v) for k, v in ins.items()}
    got = kernel(**ins)
    want = np.asarray(reference.reference(**ins))
    err = np.linalg.norm(got - want) / np.linalg.norm(want)
    print("rel_l2", err)
